# version 26
# baseline (speedup 1.0000x reference)
import numpy as np
import ml_dtypes
import concourse.bass as bass
import concourse.mybir as mybir
from concourse.bass_utils import run_bass_kernel_spmd
from contextlib import ExitStack
from concourse.bass_core_rust import IndirectOffsetOnAxis

F32 = mybir.dt.float32
BF16 = mybir.dt.bfloat16
AF = mybir.ActivationFunctionType
ALU = mybir.AluOpType
AX = mybir.AxisListType.X

P = 128
TOK = 2048
NR = 4
WT = 512
DC = 8
NG = 16
NE = 8
NQ = 4
NJ = 8
EPS = 1e-5
SCALE = 1.0 / 32.0
BIGF = 1048576.0

ENGINES = ("sync", "gpsimd", "tensor", "scalar", "vector")


class Planner:
    AUTO_SERIAL = {"vector": "sDVE", "scalar": "sACT"}

    def __init__(self, sems):
        self.q = {e: [] for e in ENGINES}
        self.sems = sems
        self.n = {s: 0 for s in sems}
        self.meta = {e: [] for e in ENGINES}
        self.bank_guard = {}
        self.buf_guard = {}

    def wait(self, eng, sem, val):
        if val <= 0:
            return
        h = self.sems[sem]

        def f(e, h=h, val=val):
            e.wait_ge(h, val)
        self.q[eng].append(f)
        self.meta[eng].append((((sem, val),), ()))

    def op(self, eng, build, inc=None, amt=1, selfwait=True, swlag=1):
        ser = self.AUTO_SERIAL.get(eng)
        if ser is not None:
            self.wait(eng, ser, self.n[ser])
            if inc is None:
                inc, amt = ser, 1
            else:
                assert inc == ser, (eng, inc)
        if inc is None:
            def f(e, build=build):
                build(e)
            self.q[eng].append(f)
            self.meta[eng].append(((), ()))
            return None
        self.n[inc] += amt
        val = self.n[inc]
        h = self.sems[inc]

        def f(e, build=build, h=h, amt=amt):
            build(e).then_inc(h, amt)
        self.q[eng].append(f)
        self.meta[eng].append(((), ((inc, amt),)))
        if ser is None and selfwait:
            self.wait(eng, inc, val - swlag * amt)
        return val

    def use_banks(self, eng, banks):
        seen = set()
        for b in banks:
            g = self.bank_guard.get(b)
            if g and g not in seen:
                self.wait(eng, g[0], g[1])
                seen.add(g)

    def consume_banks(self, banks, sem, val):
        for b in banks:
            self.bank_guard[b] = (sem, val)

    def use_buf(self, eng, key):
        g = self.buf_guard.get(key)
        if g:
            self.wait(eng, g[0], g[1])

    def hold_buf(self, key, sem, val):
        self.buf_guard[key] = (sem, val)

    def check(self):
        idx = {e: 0 for e in ENGINES}
        cur = {s: 0 for s in self.n}
        progressed = True
        while progressed:
            progressed = False
            for e in ENGINES:
                while idx[e] < len(self.meta[e]):
                    waits, incs = self.meta[e][idx[e]]
                    if all(cur[s] >= v for s, v in waits):
                        for s, a in incs:
                            cur[s] += a
                        idx[e] += 1
                        progressed = True
                    else:
                        break
        stuck = {e: (idx[e], len(self.meta[e])) for e in ENGINES
                 if idx[e] < len(self.meta[e])}
        if stuck:
            detail = {}
            for e in stuck:
                waits, _ = self.meta[e][idx[e]]
                detail[e] = [(s, v, cur[s]) for s, v in waits]
            raise RuntimeError(f"planner deadlock: stuck={stuck} waits={detail}")

    def replay(self, blk):
        qs = self.q

        @blk.sync
        def _(e):
            for f in qs["sync"]:
                f(e)

        @blk.gpsimd
        def _(e):
            for f in qs["gpsimd"]:
                f(e)

        @blk.tensor
        def _(e):
            for f in qs["tensor"]:
                f(e)

        @blk.scalar
        def _(e):
            for f in qs["scalar"]:
                f(e)

        @blk.vector
        def _(e):
            for f in qs["vector"]:
                f(e)


def build_nc():
    nc = bass.Bass("TRN2", target_bir_lowering=False)

    def din(name, shape, dt=F32):
        return nc.dram_tensor(name, list(shape), dt, kind="ExternalInput").ap()

    xt = din("xt", [1024, TOK])
    wq = din("wq", [1024, 1024]); wk = din("wk", [1024, 1024])
    wv = din("wv", [1024, 1024]); wo = din("wo", [1024, 1024])
    bqc = din("bqc", [P, 8]); bkc = din("bkc", [P, 8])
    bvc = din("bvc", [P, 8]); boc = din("boc", [P, 8])
    wrt = din("wrt", [P, 8, 8]); brcol = din("brcol", [8, 1])
    b1c = din("b1c", [P, 256]); b2c = din("b2c", [P, 64])
    w1t = din("w1t", [NE, 1024, 4096]); w2t = din("w2t", [NE, 4096, 1024])
    c_identf = din("c_identf", [P, P], F32)
    c_identbf = din("c_identbf", [P, P], BF16)
    c_gmsum = din("c_gmsum", [P, DC, NG], BF16)
    c_gmb = din("c_gmb", [NG, DC, P], BF16)
    c_sel = din("c_sel", [8, NE, P], BF16)
    c_onescol = din("c_onescol", [P, 1], F32)
    c_onesrow = din("c_onesrow", [1, P], F32)
    c_onesrow8 = din("c_onesrow8", [1, 8], F32)
    c_eps = din("c_eps", [1, 1], F32)
    c_iota16 = din("c_iota16", [P, 16], mybir.dt.int32)
    c_idpf = din("c_idpf", [P, 3], mybir.dt.int32)
    out_t = nc.dram_tensor("out_t", [TOK, 1024], F32, kind="ExternalOutput").ap()
    xt2d = nc.dram_tensor("xt2d", [TOK + 384, 1024], BF16).ap()
    w1b = nc.dram_tensor("w1b", [NE, 1024, 4096], BF16).ap()
    w2b = nc.dram_tensor("w2b", [NE, 4096, 1024], BF16).ap()
    idxd = nc.dram_tensor("idxd", [TOK + 384, 1], mybir.dt.int32).ap()

    xt_r = xt.rearrange("(c p) t -> p c t", p=P)

    with ExitStack() as st:
        E = st.enter_context
        S = lambda name, shape, dt: E(nc.sbuf_tensor(name, list(shape), dt))
        OUT = S("OUT", [P, DC, WT], F32)
        XLN2 = S("XLN2", [P, DC, WT], BF16)
        WB = S("WB", [P, 2, DC, 1024], BF16)
        W2B = S("W2B", [P, 2, NJ, 512], BF16)
        AWB = S("AWB", [P, DC, 1024], BF16)
        CAP = 384
        NCH = CAP // 128
        HB = S("HB", [P, NJ, CAP], BF16)
        POS = S("POS", [8, TOK], F32)
        SLT = S("SLT", [P, 16, 8], mybir.dt.int32)
        IDXC = S("IDXC", [P, 2, NCH], mybir.dt.int32)
        IOT16 = S("IOT16", [P, 16], mybir.dt.int32)
        IDPF = S("IDPF", [P, NCH], mybir.dt.int32)
        TMB = S("TMB", [P, 1024], BF16)
        XGT = S("XGT", [P, NCH, 1024], BF16)
        XGF = S("XGF", [P, 2, DC, CAP], BF16)
        YFM = S("YFM", [P, DC, CAP], F32)
        YTM = S("YTM", [P, 2, 1024], F32)
        TMBF = YTM[:, 0, :]
        QB = S("QB", [P, DC, WT], BF16)
        KDUP = S("KDUP", [P, NG, WT], BF16)
        VB = S("VB", [P, DC, WT], BF16)
        SA = S("SA", [P, DC, WT], BF16)
        EXPS = SA[:, 0:2, :]
        EXPS4 = SA[:, 0:4, :]
        ZC = S("ZC", [P, 2, 8], F32)
        ZACC = S("ZACC", [P, 4, NG], F32)
        DGS = S("DGS", [NG, WT], BF16)
        WRS = DGS
        DGT = S("DGT", [P, 4, NG], BF16)
        IDB = S("IDB", [P, P], BF16)
        SQB = S("SQB", [P, 1, WT], F32)
        XT1 = S("XT1", [P, 1, WT], F32)
        LGT = XT1[0:8, 0, :]
        ROWS = S("ROWS", [1, 3, WT], F32)
        RMX = S("RMX", [P, 1], F32)
        MKT = S("MKT", [P, 8], F32)
        MROW = S("MROW", [8, NR, WT], BF16)
        IDF = S("IDF", [P, P], F32)
        GMS = S("GMS", [P, DC, NG], BF16)
        GMB = S("GMB", [NG, DC, P], BF16)
        ONEC = S("ONEC", [P, 1], F32)
        ONER = S("ONER", [1, P], F32)
        ONER8 = ONER[0:1, 0:8]
        WRT = S("WRT", [P, 8, 8], F32)
        SMALL = S("SMALL", [8, 8], F32)
        BRC = SMALL[0:8, 0:1]
        BV = S("BV", [P, 8], F32); BO = S("BO", [P, 8], F32)
        BQS = S("BQS", [P, 8], F32); BKS = S("BKS", [P, 8], F32)
        B1C = S("B1C", [P, 256], F32)
        B2C = S("B2C", [P, 64], F32)
        SCOL = SMALL[0:8, 1:2]
        NSCOL = SMALL[0:8, 2:3]
        EPSC = SMALL[0:1, 3:4]

        PS = E(nc.psum_tensor("PS", [P, 8, 512], F32))

        sems = {}
        for snm in ("sINIT", "sX", "sKD", "sWA", "sW0", "sW1", "sW2", "sW3",
                    "sIX", "sGA", "sSC", "sT2D", "sXB", "sCST",
                    "sPE", "sACT", "sDVE", "sOUT"):
            sems[snm] = E(nc.semaphore(snm))
        blk = E(nc.Block())
        pl = Planner(sems)
        attn_state = {}

        init_loads = [
            (IDF, c_identf), (IDB, c_identbf), (GMS, c_gmsum), (GMB, c_gmb),
            (ONEC, c_onescol), (ONER, c_onesrow),
            (WRT, wrt), (BRC, brcol), (BQS, bqc), (BKS, bkc), (BV, bvc),
            (BO, boc), (B1C, b1c), (B2C, b2c), (EPSC, c_eps),
            (IOT16, c_iota16), (IDPF, c_idpf),
        ]
        for sb, src in init_loads:
            pl.op("sync", lambda e, sb=sb, src=src: e.dma_start(out=sb[:], in_=src),
                  inc="sINIT", amt=16, selfwait=False)
        v_init = pl.n["sINIT"]
        pl.wait("vector", "sINIT", v_init)
        pl.op("vector", lambda e: e.memset(
            XGT[:].rearrange("p c b -> p (c b)"), 0.0))
        vz = pl.op("vector", lambda e: e.memset(
            XGF[:].rearrange("p a c t -> p (a c t)"), 0.0))
        pl.wait("sync", "sDVE", vz)
        for j in range(NCH):
            pl.op("sync", lambda e, j=j: e.dma_start(
                out=xt2d[TOK + j * 128:TOK + (j + 1) * 128, :], in_=XGT[:, 0, :]),
                inc="sT2D", amt=16, selfwait=False)
        pl.op("vector", lambda e: e.tensor_scalar_mul(BQS[:], BQS[:], SCALE))
        pl.op("vector", lambda e: e.tensor_scalar_mul(BKS[:], BKS[:], SCALE))
        pl.wait("tensor", "sINIT", v_init)
        for kc in range(DC):
            v = pl.op("tensor", lambda e, kc=kc: e.matmul(
                PS[0:8, 0, 0:1], WRT[:, kc, :], ONEC[:],
                start=(kc == 0), stop=(kc == DC - 1)),
                inc="sPE" if kc == DC - 1 else None)
        pl.wait("vector", "sPE", v)
        pl.op("vector", lambda e: e.tensor_copy(SCOL[:], PS[0:8, 0, 0:1]))
        v = pl.op("vector", lambda e: e.tensor_scalar_mul(NSCOL[:], SCOL[:], -1.0))
        pl.consume_banks([0], "sDVE", v)

        def emit_ln(dst_bf, src3, banks, pe_wait, save_rows):
            bm, bs, bb1, bb2 = banks
            pl.use_banks("tensor", [bm, bs])
            if pe_wait is not None:
                pl.wait("tensor", pe_wait[0], pe_wait[1])
            for c in range(DC):
                par = c % 2
                sqt = SQB[:, 0, :] if par == 0 else XT1[:, 0, :]
                pl.use_buf("vector", ("SQB", par))
                v = pl.op("vector", lambda e, c=c, sqt=sqt: e.tensor_tensor(
                    sqt, src3[:, c, :], src3[:, c, :], ALU.mult))
                pl.wait("tensor", "sDVE", v)
                pl.op("tensor", lambda e, c=c, bm=bm: e.matmul(
                    PS[0:1, bm, :], ONEC[:], src3[:, c, :],
                    start=(c == 0), stop=(c == DC - 1)))
                vp = pl.op("tensor", lambda e, c=c, sqt=sqt, bs=bs: e.matmul(
                    PS[0:1, bs, :], ONEC[:], sqt,
                    start=(c == 0), stop=(c == DC - 1)),
                    inc="sPE")
                pl.hold_buf(("SQB", par), "sPE", vp)
            v_stats = pl.n["sPE"]
            pl.wait("vector", "sPE", v_stats)
            pl.op("vector", lambda e, bm=bm: e.tensor_scalar_mul(
                ROWS[0:1, 0, :], PS[0:1, bm, :], 1.0 / 1024.0))
            pl.op("vector", lambda e: e.tensor_tensor(
                ROWS[0:1, 1, :], ROWS[0:1, 0, :], ROWS[0:1, 0, :], ALU.mult))
            v = pl.op("vector", lambda e, bs=bs: e.scalar_tensor_tensor(
                ROWS[0:1, 1, :], PS[0:1, bs, :], 1.0 / 1024.0, ROWS[0:1, 1, :],
                ALU.mult, ALU.subtract))
            pl.consume_banks([bm, bs], "sDVE", v)
            pl.wait("scalar", "sDVE", v)
            pl.op("scalar", lambda e: e.activation(
                ROWS[0:1, 1, :], ROWS[0:1, 1, :], AF.Ln, bias=EPSC[0:1, 0:1]))
            va = pl.op("scalar", lambda e: e.activation(
                ROWS[0:1, 2, :], ROWS[0:1, 1, :], AF.Exp, scale=-0.5))
            if save_rows:
                pl.wait("vector", "sACT", va)
                pl.op("vector", lambda e: e.tensor_tensor(
                    ROWS[0:1, 1, :], ROWS[0:1, 0, :], ROWS[0:1, 2, :], ALU.mult))
            pl.wait("tensor", "sACT", va)
            pl.use_banks("tensor", [bb1, bb2])
            pl.op("tensor", lambda e, bb1=bb1: e.matmul(
                PS[:, bb1, :], ONER[:], ROWS[0:1, 0, :], start=True, stop=True))
            v = pl.op("tensor", lambda e, bb2=bb2: e.matmul(
                PS[:, bb2, :], ONER[:], ROWS[0:1, 2, :], start=True, stop=True),
                inc="sPE")
            pl.wait("vector", "sPE", v)
            for c in range(DC):
                pl.op("vector", lambda e, c=c, bb1=bb1: e.tensor_tensor(
                    XT1[:, 0, :], src3[:, c, :], PS[:, bb1, :], ALU.subtract))
                v = pl.op("vector", lambda e, c=c, bb2=bb2: e.tensor_tensor(
                    dst_bf[:, c, :], XT1[:, 0, :], PS[:, bb2, :], ALU.mult))
            pl.consume_banks([bb1, bb2], "sDVE", v)
            return v

        def emit_router(rr, v_ln2):
            sl = rr
            pl.use_banks("tensor", [4, 5, 6])
            pl.wait("tensor", "sDVE", v_ln2)
            for kc in range(DC):
                pl.op("tensor", lambda e, kc=kc, sl=sl: e.matmul(
                    PS[0:8, 4, :], WRT[:, kc, :], OUT[:, kc, :],
                    start=(kc == 0), stop=(kc == DC - 1)))
            pl.op("tensor", lambda e: e.matmul(
                PS[0:8, 5, :], ONER8[:], ROWS[0:1, 2, :], start=True, stop=True))
            v = pl.op("tensor", lambda e: e.matmul(
                PS[0:8, 6, :], ONER8[:], ROWS[0:1, 1, :], start=True, stop=True),
                inc="sPE")
            pl.wait("vector", "sPE", v)
            pl.op("vector", lambda e: e.tensor_copy(LGT[:], PS[0:8, 4, :]))
            pl.op("vector", lambda e: e.tensor_tensor(
                LGT[:], LGT[:], PS[0:8, 5, :], ALU.mult))
            pl.op("vector", lambda e: e.scalar_tensor_tensor(
                LGT[:], PS[0:8, 6, :], NSCOL[:], LGT[:], ALU.mult, ALU.add))
            v = pl.op("vector", lambda e: e.tensor_scalar(
                LGT[:], LGT[:], BRC[:], None, ALU.add))
            pl.consume_banks([4, 5, 6], "sDVE", v)
            pl.use_buf("vector", ("MROW", sl))
            for tc in range(4):
                ts = slice(tc * P, (tc + 1) * P)
                pl.use_banks("tensor", [7])
                pl.wait("tensor", "sDVE", v)
                vt = pl.op("tensor", lambda e, ts=ts: e.transpose(
                    PS[:, 7, 0:8], LGT[0:8, ts], IDF[0:8, 0:8]), inc="sPE")
                pl.wait("vector", "sPE", vt)
                vv = pl.op("vector", lambda e: e.reduce_max(
                    RMX[:], PS[:, 7, 0:8], axis=AX))
                pl.wait("vector", "sDVE", vv)
                vv = pl.op("vector", lambda e: e.tensor_scalar(
                    MKT[:], PS[:, 7, 0:8], RMX[:], None, ALU.is_ge))
                pl.consume_banks([7], "sDVE", vv)
                pl.use_banks("tensor", [7])
                pl.wait("tensor", "sDVE", vv)
                vt = pl.op("tensor", lambda e: e.transpose(
                    PS[0:8, 7, 0:P], MKT[:], IDF[:]), inc="sPE")
                pl.wait("vector", "sPE", vt)
                vv = pl.op("vector", lambda e, sl=sl, tc=tc: e.tensor_copy(
                    MROW[0:8, sl, tc * P:(tc + 1) * P], PS[0:8, 7, 0:P]))
                pl.consume_banks([7], "sDVE", vv)
            attn_state[("mrow", sl)] = pl.n["sDVE"]


        def emit_gather_dma(e_i):
            es = e_i % 2
            pl.wait("gpsimd", "sDVE", attn_state["slt"])
            pl.wait("gpsimd", "sIX", pl.n["sIX"])
            pl.op("gpsimd", lambda e: e.dma_start(
                out=idxd[0:CAP, :].rearrange("(p a) x -> p (a x)", p=P), in_=IDPF[:]),
                inc="sIX", amt=16, selfwait=False)
            vix = pl.n["sIX"]
            pl.wait("gpsimd", "sIX", vix)
            for tc in range(16):
                pl.op("gpsimd", lambda e, tc=tc, e_i=e_i: e.indirect_dma_start(
                    out=idxd, out_offset=IndirectOffsetOnAxis(SLT[:, tc, e_i:e_i + 1], 0),
                    in_=IOT16[:, tc:tc + 1], in_offset=None),
                    inc="sIX", amt=16, selfwait=False)
            vix = pl.n["sIX"]
            pl.wait("gpsimd", "sIX", vix)
            pl.use_buf("gpsimd", ("IDXC", es))
            for ch in range(NCH):
                pl.op("gpsimd", lambda e, es=es, ch=ch: e.dma_start(
                    out=IDXC[:, es, ch:ch + 1], in_=idxd[ch * 128:(ch + 1) * 128, :]),
                    inc="sIX", amt=16, selfwait=False)
            vix = pl.n["sIX"]
            pl.wait("gpsimd", "sIX", vix)
            pl.wait("gpsimd", "sT2D", pl.n["sT2D"])
            pl.use_buf("gpsimd", ("XGT",))
            pl.wait("gpsimd", "sGA", pl.n["sGA"])
            for ch in range(NCH):
                pl.op("gpsimd", lambda e, es=es, ch=ch: e.indirect_dma_start(
                    out=XGT[:, ch, :], out_offset=None,
                    in_=xt2d, in_offset=IndirectOffsetOnAxis(IDXC[:, es, ch:ch + 1], 0)),
                    inc="sGA", amt=16, selfwait=False)
            attn_state[("ga", e_i)] = pl.n["sGA"]

        def emit_gather_tp(e_i):
            es = e_i % 2
            v_ga = attn_state[("ga", e_i)]
            PSB4 = PS[:, 0, :].bitcast(BF16)
            pl.use_buf("tensor", ("XGF", es))
            for ch in range(NCH):
                pl.use_banks("tensor", [0])
                pl.wait("tensor", "sGA", v_ga)
                vpe = None
                for c in range(DC):
                    vpe = pl.op("tensor", lambda e, ch=ch, c=c, PSB4=PSB4: e.transpose(
                        PSB4[:, c * 128:(c + 1) * 128],
                        XGT[:, ch, c * 128:(c + 1) * 128], IDB[:]),
                        inc="sPE" if c == DC - 1 else None)
                pl.wait("vector", "sPE", vpe)
                vv = pl.op("vector", lambda e, es=es, ch=ch, PSB4=PSB4: e.tensor_copy(
                    XGF[:, es, :, ch * 128:(ch + 1) * 128],
                    PSB4[:, 0:1024].rearrange("p (c t) -> p c t", c=DC)))
                pl.consume_banks([0], "sDVE", vv)
            pl.hold_buf(("XGT",), "sPE", pl.n["sPE"])
            attn_state[("xg", e_i)] = vv
            return vv

        def weight_caster():
            for e_i in range(NE):
                for t, dst, rch in ((w1t, w1b, 128), (w2t, w2b, 512)):
                    for ch in range(8):
                        pl.op("gpsimd", lambda e, t=t, dst=dst, e_i=e_i,
                              ch=ch, rch=rch: e.dma_start(
                                  out=dst[e_i][ch * rch:(ch + 1) * rch, :],
                                  in_=t[e_i][ch * rch:(ch + 1) * rch, :]),
                              inc="sCST", amt=16, selfwait=False)
                        yield 1

        def attn_units(w, solo=False):
            pl.use_buf("sync", ("OUT", "pe"))
            pl.use_buf("sync", ("OUT", "dve"))
            pl.wait("sync", "sX", pl.n["sX"])
            vx = pl.op("sync", lambda e, w=w: e.dma_start(
                out=OUT[:], in_=xt_r[:, :, w * WT:(w + 1) * WT]),
                inc="sX", amt=16, selfwait=False)
            pl.use_buf("gpsimd", ("AWB",))
            pl.wait("gpsimd", "sWA", pl.n["sWA"])
            vwa = pl.op("gpsimd", lambda e: e.dma_start(
                out=AWB[:], in_=wq.rearrange("(c p) m -> p c m", p=P)),
                inc="sWA", amt=16, selfwait=False)
            yield 0.5
            pl.use_buf("vector", ("SA",))
            pl.wait("vector", "sX", vx)
            v_xln1 = emit_ln(SA, OUT[:], (6, 7, 6, 7), ("sX", vx), False)
            yield 2.0

            def proj(vw, dst_kind):
                pl.wait("tensor", "sWA", vw)
                pl.wait("tensor", "sDVE", v_xln1)
                vlast = vv = None
                for mc in range(DC):
                    bank = (2, 3, 6, 7)[mc % 4]
                    pl.use_banks("tensor", [bank])
                    for kc in range(DC):
                        vpe = pl.op("tensor",
                                    lambda e, mc=mc, kc=kc, bank=bank: e.matmul(
                                        PS[:, bank, :],
                                        AWB[:, kc, mc * 128:(mc + 1) * 128],
                                        SA[:, kc, :],
                                        start=(kc == 0), stop=(kc == DC - 1)),
                                    inc="sPE" if kc == DC - 1 else None)
                    pl.wait("vector", "sPE", vpe)
                    if dst_kind == "Q":
                        vv = pl.op("vector", lambda e, mc=mc, bank=bank: e.tensor_scalar(
                            QB[:, mc, :], PS[:, bank, :], SCALE, BQS[:, mc:mc + 1],
                            ALU.mult, ALU.add))
                    elif dst_kind == "K":
                        pl.op("vector", lambda e, mc=mc, bank=bank: e.tensor_scalar(
                            KDUP[0:64, 2 * mc, :], PS[0:64, bank, :], SCALE,
                            BKS[0:64, mc:mc + 1], ALU.mult, ALU.add))
                        vv = pl.op("vector", lambda e, mc=mc, bank=bank: e.tensor_scalar(
                            KDUP[64:128, 2 * mc + 1, :], PS[64:128, bank, :], SCALE,
                            BKS[64:128, mc:mc + 1], ALU.mult, ALU.add))
                    else:
                        vv = pl.op("vector", lambda e, mc=mc, bank=bank: e.tensor_scalar(
                            VB[:, mc, :], PS[:, bank, :], BV[:, mc:mc + 1], None,
                            ALU.add))
                    pl.consume_banks([bank], "sDVE", vv)
                    vlast = vpe
                return vlast, vv

            vq_last, _ = proj(vwa, "Q")
            pl.wait("gpsimd", "sPE", vq_last)
            pl.wait("gpsimd", "sWA", pl.n["sWA"])
            vwa = pl.op("gpsimd", lambda e: e.dma_start(
                out=AWB[:], in_=wk.rearrange("(c p) m -> p c m", p=P)),
                inc="sWA", amt=16, selfwait=False)
            yield 2.0
            vk_last, v_kevac = proj(vwa, "K")
            pl.wait("sync", "sDVE", v_kevac)
            pl.wait("sync", "sKD", pl.n["sKD"])
            pl.op("sync", lambda e: e.dma_start(
                out=KDUP[64:128, 0:NG:2, :], in_=KDUP[0:64, 0:NG:2, :]),
                inc="sKD", amt=16, selfwait=False)
            v_kdup = pl.op("sync", lambda e: e.dma_start(
                out=KDUP[0:64, 1:NG:2, :], in_=KDUP[64:128, 1:NG:2, :]),
                inc="sKD", amt=16, selfwait=False)
            pl.wait("gpsimd", "sPE", vk_last)
            pl.wait("gpsimd", "sWA", pl.n["sWA"])
            vwa = pl.op("gpsimd", lambda e: e.dma_start(
                out=AWB[:], in_=wv.rearrange("(c p) m -> p c m", p=P)),
                inc="sWA", amt=16, selfwait=False)
            yield 2.0
            vv_last, v_qkv = proj(vwa, "V")
            pl.wait("gpsimd", "sPE", vv_last)
            pl.wait("gpsimd", "sWA", pl.n["sWA"])
            vwo = pl.op("gpsimd", lambda e: e.dma_start(
                out=AWB[:], in_=wo.rearrange("(c p) m -> p c m", p=P)),
                inc="sWA", amt=16, selfwait=False)
            attn_state["vwo"] = vwo
            yield 2.0
            pl.wait("tensor", "sDVE", v_qkv)
            pl.wait("tensor", "sKD", v_kdup)
            pc = 0
            ng_sz = 4 if solo else 2
            for gb in range(NG):
                for t0 in range(4):
                    par = (gb * 4 + t0) % 2
                    pl.use_buf("scalar", ("ZC", par))
                    for bt in range(NG // ng_sz):
                        if solo:
                            bp = (0, 4)[pc % 2]
                        else:
                            bp = (2, 6)[pc % 2]
                        banks = list(range(bp, bp + ng_sz))
                        pl.use_banks("tensor", banks)
                        for gi in range(ng_sz):
                            ga = bt * ng_sz + gi
                            vpe = pl.op("tensor",
                                        lambda e, gb=gb, t0=t0, ga=ga, bp=bp, gi=gi: e.matmul(
                                            PS[:, bp + gi, :],
                                            KDUP[(ga % 2) * 64:(ga % 2) * 64 + 64, gb,
                                                 t0 * 128:(t0 + 1) * 128],
                                            QB[(ga % 2) * 64:(ga % 2) * 64 + 64,
                                               ga // 2, :],
                                            start=True, stop=True,
                                            tile_position=((ga % 2) * 64, 0)),
                                        inc="sPE" if gi == ng_sz - 1 else None)
                        pl.wait("scalar", "sPE", vpe)
                        va = pl.op("scalar",
                                   lambda e, bp=bp, par=par, bt=bt, ng_sz=ng_sz: e.activation(
                                       (EXPS4 if ng_sz == 4 else EXPS)[:]
                                       .rearrange("p a b -> p (a b)"),
                                       PS[:, bp:bp + ng_sz, :].rearrange("p a b -> p (a b)"),
                                       AF.Exp, accum_out=ZC[:, par, bt:bt + 1]))
                        pl.consume_banks(banks, "sACT", va)
                        pc += 1
                        yield 1.29 * ng_sz / 2
                    pl.wait("vector", "sACT", pl.n["sACT"])
                    vv = pl.op("vector", lambda e, gb=gb, t0=t0, par=par,
                               nb=NG // ng_sz: e.reduce_sum(
                        ZACC[:, t0, gb:gb + 1], ZC[:, par, 0:nb], axis=AX))
                    pl.hold_buf(("ZC", par), "sDVE", vv)
            attn_state["v_scores_pe"] = pl.n["sPE"]
            attn_state["v_exp_all"] = pl.n["sACT"]

        def attn_tail(w):
            sl = w % 2
            pl.wait("vector", "sPE", attn_state["v_scores_pe"])
            pl.wait("vector", "sACT", attn_state["v_exp_all"])
            v_prod = None
            for c in range(DC):
                pl.op("vector", lambda e, c=c: e.tensor_tensor(
                    SA[0:64, c, :], QB[0:64, c, :], KDUP[0:64, 2 * c, :], ALU.mult))
                v_prod = pl.op("vector", lambda e, c=c: e.tensor_tensor(
                    SA[64:128, c, :], QB[64:128, c, :], KDUP[64:128, 2 * c + 1, :],
                    ALU.mult))
            pl.wait("tensor", "sDVE", v_prod)
            pl.use_banks("tensor", [0])
            for c in range(DC):
                vpe = pl.op("tensor", lambda e, c=c: e.matmul(
                    PS[0:16, 0, :], GMS[:, c, :], SA[:, c, :],
                    start=(c == 0), stop=(c == DC - 1)),
                    inc="sPE" if c == DC - 1 else None)
            pl.wait("vector", "sPE", vpe)
            vv = pl.op("vector", lambda e: e.tensor_copy(DGS[:], PS[0:16, 0, :]))
            pl.consume_banks([0], "sDVE", vv)
            pl.use_banks("tensor", [1])
            pl.wait("tensor", "sDVE", vv)
            PSB1 = PS[:, 1, :].bitcast(BF16)
            for t0 in range(4):
                vpe = pl.op("tensor", lambda e, t0=t0, PSB1=PSB1: e.transpose(
                    PSB1[:, t0 * NG:(t0 + 1) * NG],
                    DGS[0:16, t0 * 128:(t0 + 1) * 128], IDB[0:16, 0:16]),
                    inc="sPE" if t0 == 3 else None)
            pl.wait("scalar", "sPE", vpe)
            va = pl.op("scalar", lambda e, PSB1=PSB1: e.activation(
                DGT[:].rearrange("p a b -> p (a b)"), PSB1[:, 0:64], AF.Exp))
            pl.consume_banks([1], "sACT", va)
            pl.wait("vector", "sACT", va)
            pl.op("vector", lambda e: e.reciprocal(
                ZACC[:].rearrange("p a b -> p (a b)"),
                ZACC[:].rearrange("p a b -> p (a b)")))
            vv = pl.op("vector", lambda e: e.tensor_tensor(
                DGT[:].rearrange("p a b -> p (a b)"),
                DGT[:].rearrange("p a b -> p (a b)"),
                ZACC[:].rearrange("p a b -> p (a b)"), ALU.mult))
            pl.wait("tensor", "sDVE", vv)
            pl.use_banks("tensor", [2])
            PSB2 = PS[:, 2, :].bitcast(BF16)
            for t0 in range(4):
                vpe = pl.op("tensor", lambda e, t0=t0, PSB2=PSB2: e.transpose(
                    PSB2[0:16, t0 * 128:(t0 + 1) * 128], DGT[:, t0, :], IDB[:]),
                    inc="sPE" if t0 == 3 else None)
            pl.wait("vector", "sPE", vpe)
            for t0 in range(4):
                vv = pl.op("vector", lambda e, t0=t0, PSB2=PSB2: e.tensor_copy(
                    WRS[0:16, t0 * 128:(t0 + 1) * 128],
                    PSB2[0:16, t0 * 128:(t0 + 1) * 128]))
            pl.consume_banks([2], "sDVE", vv)
            v_wrows = vv
            pl.wait("tensor", "sDVE", v_wrows)
            for c in range(DC):
                bank = 3 if c % 2 == 0 else 5
                pl.use_banks("tensor", [bank])
                vpe = pl.op("tensor", lambda e, c=c, bank=bank: e.matmul(
                    PS[:, bank, :], GMB[:, c, :], WRS[:], start=True, stop=True),
                    inc="sPE")
                pl.wait("vector", "sPE", vpe)
                vv = pl.op("vector", lambda e, c=c, bank=bank: e.tensor_tensor(
                    SA[:, c, :], PS[:, bank, :], VB[:, c, :], ALU.mult))
                pl.consume_banks([bank], "sDVE", vv)
            v_wv = vv
            pl.wait("tensor", "sWA", attn_state["vwo"])
            pl.wait("tensor", "sDVE", v_wv)
            for mc in range(DC):
                bank = 6 + mc % 2
                pl.use_banks("tensor", [bank])
                for kc in range(DC):
                    vpe = pl.op("tensor", lambda e, mc=mc, kc=kc, bank=bank: e.matmul(
                        PS[:, bank, :], AWB[:, kc, mc * 128:(mc + 1) * 128],
                        SA[:, kc, :], start=(kc == 0), stop=(kc == DC - 1)),
                        inc="sPE" if kc == DC - 1 else None)
                pl.wait("vector", "sPE", vpe)
                vv = pl.op("vector", lambda e, mc=mc, bank=bank:
                           e.scalar_tensor_tensor(
                               OUT[:, mc, :], PS[:, bank, :], BO[:, mc:mc + 1],
                               OUT[:, mc, :], ALU.add, ALU.add))
                pl.consume_banks([bank], "sDVE", vv)
                if mc == DC - 1:
                    pl.hold_buf(("AWB",), "sPE", vpe)
            v_x1 = vv
            pl.hold_buf(("SA",), "sPE", pl.n["sPE"])
            pl.wait("tensor", "sDVE", v_x1)
            PSRF = PS[:, 6:8, :]
            for tc in range(4):
                pl.use_banks("tensor", [6, 7])
                vpe = None
                for c in range(DC):
                    vpe = pl.op("tensor", lambda e, tc=tc, c=c, PSRF=PSRF: e.transpose(
                        PSRF[:, c // 4, (c % 4) * 128:(c % 4 + 1) * 128],
                        OUT[:, c, tc * 128:(tc + 1) * 128], IDF[:]),
                        inc="sPE" if c == DC - 1 else None)
                pl.wait("vector", "sPE", vpe)
                pl.wait("vector", "sXB", pl.n["sXB"])
                vv = pl.op("vector", lambda e, PSRF=PSRF: e.tensor_copy(
                    TMBF[:], PSRF[:, :, :].rearrange("p a t -> p (a t)")))
                pl.consume_banks([6, 7], "sDVE", vv)
                pl.wait("sync", "sDVE", vv)
                pl.wait("sync", "sXB", pl.n["sXB"])
                pl.op("sync", lambda e, w=w, tc=tc: e.dma_start(
                    out=out_t[w * WT + tc * 128:w * WT + (tc + 1) * 128, :],
                    in_=TMBF[:]),
                    inc="sXB", amt=16, selfwait=False)
            v_x1pe = pl.n["sPE"]
            pl.use_buf("vector", ("XLN2",))
            v_ln2 = emit_ln(XLN2[:], OUT[:], (0, 1, 2, 3),
                            ("sDVE", v_x1), True)
            emit_router(w, v_ln2)
            PSB3 = PS[:, 2, :].bitcast(BF16)
            for tc in range(4):
                pl.use_banks("tensor", [2])
                vpe = None
                for c in range(DC):
                    vpe = pl.op("tensor", lambda e, tc=tc, c=c, PSB3=PSB3: e.transpose(
                        PSB3[:, c * 128:(c + 1) * 128],
                        XLN2[:, c, tc * P:(tc + 1) * P], IDB[:]),
                        inc="sPE" if c == DC - 1 else None)
                pl.wait("vector", "sPE", vpe)
                pl.wait("vector", "sT2D", pl.n["sT2D"])
                vv = pl.op("vector", lambda e, PSB3=PSB3: e.tensor_copy(
                    TMB[:], PSB3[:, 0:1024]))
                pl.consume_banks([2], "sDVE", vv)
                pl.wait("sync", "sDVE", vv)
                pl.wait("sync", "sT2D", pl.n["sT2D"])
                pl.op("sync", lambda e, w=w, tc=tc: e.dma_start(
                    out=xt2d[w * WT + tc * P:w * WT + (tc + 1) * P, :], in_=TMB[:]),
                    inc="sT2D", amt=16, selfwait=False)
            pl.hold_buf(("XLN2",), "sPE", pl.n["sPE"])
            pl.hold_buf(("OUT", "pe"), "sPE", pl.n["sPE"])
            pl.hold_buf(("OUT", "dve"), "sDVE", pl.n["sDVE"])

        def slot_build():
            MRV = MROW[0:8, :, :].rearrange("p a b -> p (a b)")
            ZROW = XGT[0:8, :, :].rearrange("p a b -> p (a b)")[:, 0:TOK]
            POSN = PS[0:8, 4:8, :].rearrange("p a b -> p (a b)")
            pl.use_banks("vector", [4, 5, 6, 7])
            pl.op("vector", lambda e, MRV=MRV, ZROW=ZROW: e.tensor_tensor_scan(
                POS[:], MRV, ZROW, 0.0, ALU.add, ALU.add))
            pl.op("vector", lambda e, MRV=MRV, POSN=POSN: e.tensor_scalar(
                POSN, MRV, -1.0, 1.0, ALU.mult, ALU.add))
            pl.op("vector", lambda e, ZROW=ZROW, POSN=POSN: e.tensor_tensor_scan(
                POSN, POSN, ZROW, 0.0, ALU.add, ALU.add))
            pl.op("vector", lambda e, POSN=POSN: e.tensor_scalar(
                POSN, POSN, CAP - 1.0, None, ALU.add))
            pl.op("vector", lambda e: e.tensor_scalar(
                POS[:], POS[:], -1.0, None, ALU.add))
            pl.op("vector", lambda e, POSN=POSN: e.tensor_tensor(
                POS[:], POS[:], POSN, ALU.subtract))
            pl.op("vector", lambda e, MRV=MRV: e.tensor_tensor(
                POS[:], POS[:], MRV, ALU.mult))
            v_pos = pl.op("vector", lambda e, POSN=POSN: e.tensor_tensor(
                POS[:], POS[:], POSN, ALU.add))
            pl.consume_banks([4, 5, 6, 7], "sDVE", v_pos)
            pl.wait("tensor", "sDVE", v_pos)
            for tc in range(16):
                bank = tc % 2
                pl.use_banks("tensor", [bank])
                vpe = pl.op("tensor", lambda e, tc=tc, bank=bank: e.transpose(
                    PS[0:128, bank, 0:8], POS[0:8, tc * P:(tc + 1) * P],
                    IDF[0:8, 0:8]), inc="sPE")
                pl.wait("vector", "sPE", vpe)
                vv = pl.op("vector", lambda e, tc=tc, bank=bank: e.tensor_copy(
                    SLT[:, tc, :], PS[:, bank, 0:8]))
                pl.consume_banks([bank], "sDVE", vv)
            attn_state["slt"] = pl.n["sDVE"]

        def ffn_expert(e_i):
            if True:
                if e_i + 1 < NE:
                    emit_gather_dma(e_i + 1)
                vm = attn_state[("xg", e_i)]
                es = e_i % 2
                for q in range(NQ):
                    sq = q % 2
                    pl.use_buf("sync", ("WB", sq))
                    s1nm = "sW0" if sq == 0 else "sW1"
                    pl.wait("sync", s1nm, pl.n[s1nm])
                    pl.wait("sync", "sCST", 16 * 16 * NE)
                    vw1 = pl.op("sync", lambda e, e_i=e_i, q=q, sq=sq: e.dma_start(
                        out=WB[:, sq, :, :],
                        in_=w1b[e_i].rearrange("(c p) f -> p c f", p=P)
                            [:, :, q * 1024:(q + 1) * 1024]),
                        inc=s1nm, amt=16, selfwait=False)
                    vw2 = [None, None]
                    for hh in range(2):
                        pl.use_buf("sync", ("W2B", hh))
                        s2nm = "sW2" if hh == 0 else "sW3"
                        pl.wait("sync", s2nm, pl.n[s2nm])
                        vw2[hh] = pl.op("sync",
                                        lambda e, e_i=e_i, q=q, hh=hh: e.dma_start(
                                            out=W2B[:, hh, :, :],
                                            in_=w2b[e_i].rearrange("(j p) d -> p j d", p=P)
                                                [:, q * NJ:(q + 1) * NJ,
                                                 hh * 512:(hh + 1) * 512]),
                                        inc=s2nm, amt=16, selfwait=False)
                    hevac = [("sDVE", 0)] * NJ
                    pl.wait("tensor", s1nm, vw1)
                    pl.wait("tensor", "sDVE", vm)
                    vv = vpe = None
                    for j in range(NJ):
                        hbank = 2 + j % 2
                        pl.use_banks("tensor", [hbank])
                        for kc in range(DC):
                            vpe = pl.op("tensor",
                                        lambda e, j=j, kc=kc, hbank=hbank, sq=sq, es=es: e.matmul(
                                            PS[:, hbank, 0:CAP],
                                            WB[:, sq, kc, j * 128:(j + 1) * 128],
                                            XGF[:, es, kc, :],
                                            start=(kc == 0), stop=(kc == DC - 1)),
                                        inc="sPE" if kc == DC - 1 else None)
                        if j % 2 == 0:
                            pl.wait("vector", "sPE", vpe)
                            vv = pl.op("vector",
                                       lambda e, j=j, q=q, e_i=e_i, hbank=hbank: e.tensor_scalar(
                                           HB[:, j, :], PS[:, hbank, 0:CAP],
                                           B1C[:, e_i * 32 + q * NJ + j:
                                               e_i * 32 + q * NJ + j + 1],
                                           0.0, ALU.add, ALU.max))
                            pl.consume_banks([hbank], "sDVE", vv)
                            hevac[j] = ("sDVE", vv)
                        else:
                            pl.wait("scalar", "sPE", vpe)
                            vv = pl.op("scalar",
                                       lambda e, j=j, q=q, e_i=e_i, hbank=hbank: e.activation(
                                           HB[:, j, :], PS[:, hbank, 0:CAP], AF.Relu,
                                           bias=B1C[:, e_i * 32 + q * NJ + j:
                                                    e_i * 32 + q * NJ + j + 1]))
                            pl.consume_banks([hbank], "sACT", vv)
                            hevac[j] = ("sACT", vv)
                    pl.hold_buf(("WB", sq), "sPE", vpe)
                    for dr in range(4):
                        hh = dr // 2
                        pl.wait("tensor", "sW2" if hh == 0 else "sW3", vw2[hh])
                        yb = 0 if dr % 2 == 0 else 4
                        ybanks = [yb, yb + 1]
                        pl.use_banks("tensor", ybanks)
                        v_y = None
                        for j in range(NJ):
                            if dr == 0:
                                pl.wait("tensor", hevac[j][0], hevac[j][1])
                            for b2i in range(2):
                                dc = dr * 2 + b2i
                                v_y = pl.op("tensor",
                                            lambda e, j=j, dc=dc, b2i=b2i, hh=hh, yb=yb: e.matmul(
                                                PS[:, yb + b2i, 0:CAP],
                                                W2B[:, hh, j,
                                                    (dc % 4) * 128:(dc % 4 + 1) * 128],
                                                HB[:, j, :],
                                                start=(j == 0), stop=(j == NJ - 1)),
                                            inc="sPE" if (j == NJ - 1 and b2i == 1)
                                            else None)
                        pl.wait("vector", "sPE", v_y)
                        if q == 0:
                            for b2i in range(2):
                                dc = dr * 2 + b2i
                                vv = pl.op("vector",
                                           lambda e, dc=dc, yb=yb, b2i=b2i, e_i=e_i: e.tensor_scalar(
                                               YFM[:, dc, :], PS[:, yb + b2i, 0:CAP],
                                               B2C[:, e_i * 8 + dc:e_i * 8 + dc + 1],
                                               None, ALU.add))
                        else:
                            vv = pl.op("vector", lambda e, dr=dr, yb=yb: e.tensor_tensor(
                                YFM[:, dr * 2:(dr + 1) * 2, :],
                                YFM[:, dr * 2:(dr + 1) * 2, :],
                                PS[:, yb:yb + 2, 0:CAP], ALU.add))
                        pl.consume_banks(ybanks, "sDVE", vv)
                    pl.hold_buf(("W2B", 0), "sPE", pl.n["sPE"])
                    pl.hold_buf(("W2B", 1), "sPE", pl.n["sPE"])

                pl.hold_buf(("XGF", es), "sPE", pl.n["sPE"])
                PSY = PS[:, 4:6, :]
                for ch in range(NCH):
                    cp = ch % 2
                    pl.use_banks("tensor", [4, 5])
                    vpe = None
                    for c in range(DC):
                        vpe = pl.op("tensor", lambda e, ch=ch, c=c: e.transpose(
                            PSY[:, c // 4, (c % 4) * 128:(c % 4 + 1) * 128],
                            YFM[:, c, ch * 128:(ch + 1) * 128], IDF[:]),
                            inc="sPE" if c == DC - 1 else None)
                    pl.wait("vector", "sPE", vpe)
                    pl.use_buf("vector", ("YTM", cp))
                    vv = pl.op("vector", lambda e, cp=cp: e.tensor_copy(
                        YTM[:, cp, :],
                        PSY[:, :, :].rearrange("p a t -> p (a t)")))
                    pl.consume_banks([4, 5], "sDVE", vv)
                    pl.wait("gpsimd", "sDVE", vv)
                    pl.wait("gpsimd", "sXB", 16 * 16)
                    pl.wait("gpsimd", "sSC", pl.n["sSC"])
                    vsc = pl.op("gpsimd", lambda e, es=es, ch=ch, cp=cp: e.indirect_dma_start(
                        out=out_t, out_offset=IndirectOffsetOnAxis(IDXC[:, es, ch:ch + 1], 0),
                        in_=YTM[:, cp, :], in_offset=None,
                        bounds_check=TOK - 1, oob_is_err=False,
                        compute_op=ALU.add),
                        inc="sSC", amt=16, selfwait=False)
                    pl.hold_buf(("YTM", cp), "sSC", vsc)
                pl.hold_buf(("IDXC", es), "sSC", pl.n["sSC"])
                if e_i + 1 < NE:
                    emit_gather_tp(e_i + 1)

        caster = weight_caster()
        ticks = 0

        def tick():
            nonlocal ticks
            ticks += 1
            if ticks % 6 == 0:
                next(caster, None)

        for _ in attn_units(0, solo=True):
            tick()
        for w in range(NR):
            attn_tail(w)
            if w + 1 < NR:
                for _ in attn_units(w + 1, solo=True):
                    tick()
        for _ in caster:
            pass
        pl.hold_buf(("YTM", 0), "sXB", 16 * 4 * NR)
        slot_build()
        emit_gather_dma(0)
        emit_gather_tp(0)
        for e_i in range(NE):
            ffn_expert(e_i)
        pl.wait("sync", "sXB", 16 * 4 * NR)
        pl.wait("sync", "sSC", 16 * NCH * NE)

        pl.check()
        n_ops = {e: len(pl.q[e]) for e in ENGINES}
        pl.replay(blk)
    return nc, n_ops


_CACHE = {}


def _host_layout(inputs):
    bf = ml_dtypes.bfloat16
    f32 = np.float32
    X = np.asarray(inputs["L0_attn_X_embed"], f32)
    xt_full = np.ascontiguousarray(X.reshape(16384, 1024).T)

    def colsplit(b):
        return np.ascontiguousarray(np.asarray(b, f32).reshape(8, 128).T)

    common = dict(
        wq=np.ascontiguousarray(np.asarray(inputs["L0_attn_W_q"], f32)),
        wk=np.ascontiguousarray(np.asarray(inputs["L0_attn_W_k"], f32)),
        wv=np.ascontiguousarray(np.asarray(inputs["L0_attn_W_v"], f32)),
        wo=np.ascontiguousarray(np.asarray(inputs["L0_attn_W_o"], f32)),
        bqc=colsplit(inputs["L0_attn_b_q"]),
        bkc=colsplit(inputs["L0_attn_b_k"]),
        bvc=colsplit(inputs["L0_attn_b_v"]),
        boc=colsplit(inputs["L0_attn_b_o"]),
        wrt=np.ascontiguousarray(
            np.asarray(inputs["L0_ffn_Wr"], f32).T.reshape(8, 128, 8)
            .transpose(1, 0, 2)),
        brcol=np.ascontiguousarray(np.asarray(inputs["L0_ffn_br"], f32).reshape(8, 1)),
        b1c=np.ascontiguousarray(
            np.asarray(inputs["L0_ffn_b1"], f32).reshape(8, 32, 128)
            .transpose(2, 0, 1).reshape(128, 256)),
        b2c=np.ascontiguousarray(
            np.asarray(inputs["L0_ffn_b2"], f32).reshape(8, 8, 128)
            .transpose(2, 0, 1).reshape(128, 64)),
        w1t=np.ascontiguousarray(np.asarray(inputs["L0_ffn_W1"], f32).transpose(0, 2, 1)),
        w2t=np.ascontiguousarray(np.asarray(inputs["L0_ffn_W2"], f32).transpose(0, 2, 1)),
    )
    gm_sum = np.zeros((P, DC, NG), bf)
    gm_b = np.zeros((NG, DC, P), bf)
    for c in range(DC):
        for p in range(P):
            g = 2 * c + p // 64
            gm_sum[p, c, g] = 1
            gm_b[g, c, p] = 1
    sel = np.zeros((8, NE, P), bf)
    for e_i in range(NE):
        sel[e_i, e_i, :] = 1
    common.update(
        c_identf=np.eye(P, dtype=f32),
        c_identbf=np.eye(P, dtype=bf),
        c_gmsum=gm_sum, c_gmb=gm_b, c_sel=sel,
        c_onescol=np.ones((P, 1), f32),
        c_onesrow=np.ones((1, P), f32),
        c_onesrow8=np.ones((1, 8), f32),
        c_eps=np.full((1, 1), EPS, f32),
        c_iota16=np.ascontiguousarray(
            (np.arange(16)[None, :] * P + np.arange(P)[:, None]).astype(np.int32)),
        c_idpf=np.ascontiguousarray(
            (TOK + 3 * np.arange(P)[:, None] + np.arange(3)[None, :])
            .astype(np.int32)),
    )
    in_maps = []
    for c in range(8):
        m = dict(common)
        m["xt"] = np.ascontiguousarray(xt_full[:, c * TOK:(c + 1) * TOK])
        in_maps.append(m)
    return in_maps


def kernel(**inputs):
    if "nc" not in _CACHE:
        _CACHE["nc"] = build_nc()[0]
    nc = _CACHE["nc"]
    in_maps = _host_layout(inputs)
    res = run_bass_kernel_spmd(nc, in_maps, core_ids=list(range(8)))
    outs = [np.asarray(r["out_t"], dtype=np.float32) for r in res.results]
    return np.ascontiguousarray(np.concatenate(outs, axis=0)).reshape(2, 8192, 1024)


# version 27
# speedup vs baseline: 1.1776x; 1.1776x over previous
import numpy as np
import ml_dtypes
import concourse.bass as bass
import concourse.mybir as mybir
from concourse.bass_utils import run_bass_kernel_spmd
from contextlib import ExitStack
from concourse.bass_core_rust import IndirectOffsetOnAxis

F32 = mybir.dt.float32
BF16 = mybir.dt.bfloat16
AF = mybir.ActivationFunctionType
ALU = mybir.AluOpType
AX = mybir.AxisListType.X

P = 128
TOK = 2048
NR = 4
WT = 512
DC = 8
NG = 16
NE = 8
NQ = 4
NJ = 8
EPS = 1e-5
SCALE = 1.0 / 32.0
BIGF = 1048576.0

ENGINES = ("sync", "gpsimd", "tensor", "scalar", "vector")


class Planner:
    AUTO_SERIAL = {"vector": "sDVE", "scalar": "sACT"}

    def __init__(self, sems):
        self.q = {e: [] for e in ENGINES}
        self.sems = sems
        self.n = {s: 0 for s in sems}
        self.meta = {e: [] for e in ENGINES}
        self.bank_guard = {}
        self.buf_guard = {}

    def wait(self, eng, sem, val):
        if val <= 0:
            return
        h = self.sems[sem]

        def f(e, h=h, val=val):
            e.wait_ge(h, val)
        self.q[eng].append(f)
        self.meta[eng].append((((sem, val),), ()))

    def op(self, eng, build, inc=None, amt=1, selfwait=True, swlag=1):
        ser = self.AUTO_SERIAL.get(eng)
        if ser is not None:
            self.wait(eng, ser, self.n[ser])
            if inc is None:
                inc, amt = ser, 1
            else:
                assert inc == ser, (eng, inc)
        if inc is None:
            def f(e, build=build):
                build(e)
            self.q[eng].append(f)
            self.meta[eng].append(((), ()))
            return None
        self.n[inc] += amt
        val = self.n[inc]
        h = self.sems[inc]

        def f(e, build=build, h=h, amt=amt):
            build(e).then_inc(h, amt)
        self.q[eng].append(f)
        self.meta[eng].append(((), ((inc, amt),)))
        if ser is None and selfwait:
            self.wait(eng, inc, val - swlag * amt)
        return val

    def use_banks(self, eng, banks):
        seen = set()
        for b in banks:
            g = self.bank_guard.get(b)
            if g and g not in seen:
                self.wait(eng, g[0], g[1])
                seen.add(g)

    def consume_banks(self, banks, sem, val):
        for b in banks:
            self.bank_guard[b] = (sem, val)

    def use_buf(self, eng, key):
        g = self.buf_guard.get(key)
        if g:
            self.wait(eng, g[0], g[1])

    def hold_buf(self, key, sem, val):
        self.buf_guard[key] = (sem, val)

    def check(self):
        idx = {e: 0 for e in ENGINES}
        cur = {s: 0 for s in self.n}
        progressed = True
        while progressed:
            progressed = False
            for e in ENGINES:
                while idx[e] < len(self.meta[e]):
                    waits, incs = self.meta[e][idx[e]]
                    if all(cur[s] >= v for s, v in waits):
                        for s, a in incs:
                            cur[s] += a
                        idx[e] += 1
                        progressed = True
                    else:
                        break
        stuck = {e: (idx[e], len(self.meta[e])) for e in ENGINES
                 if idx[e] < len(self.meta[e])}
        if stuck:
            detail = {}
            for e in stuck:
                waits, _ = self.meta[e][idx[e]]
                detail[e] = [(s, v, cur[s]) for s, v in waits]
            raise RuntimeError(f"planner deadlock: stuck={stuck} waits={detail}")

    def replay(self, blk):
        qs = self.q

        @blk.sync
        def _(e):
            for f in qs["sync"]:
                f(e)

        @blk.gpsimd
        def _(e):
            for f in qs["gpsimd"]:
                f(e)

        @blk.tensor
        def _(e):
            for f in qs["tensor"]:
                f(e)

        @blk.scalar
        def _(e):
            for f in qs["scalar"]:
                f(e)

        @blk.vector
        def _(e):
            for f in qs["vector"]:
                f(e)


def build_nc():
    nc = bass.Bass("TRN2", target_bir_lowering=False)

    def din(name, shape, dt=F32):
        return nc.dram_tensor(name, list(shape), dt, kind="ExternalInput").ap()

    xt = din("xt", [1024, TOK])
    wq = din("wq", [1024, 1024]); wk = din("wk", [1024, 1024])
    wv = din("wv", [1024, 1024]); wo = din("wo", [1024, 1024])
    bqc = din("bqc", [P, 8]); bkc = din("bkc", [P, 8])
    bvc = din("bvc", [P, 8]); boc = din("boc", [P, 8])
    wrt = din("wrt", [P, 8, 8]); brcol = din("brcol", [8, 1])
    b1c = din("b1c", [P, 256]); b2c = din("b2c", [P, 64])
    w1t = din("w1t", [NE, 1024, 4096]); w2t = din("w2t", [NE, 4096, 1024])
    c_identf = din("c_identf", [P, P], F32)
    c_identbf = din("c_identbf", [P, P], BF16)
    c_gmsum = din("c_gmsum", [P, DC, NG], BF16)
    c_gmb = din("c_gmb", [NG, DC, P], BF16)
    c_sel = din("c_sel", [8, NE, P], BF16)
    c_onescol = din("c_onescol", [P, 1], F32)
    c_onesrow = din("c_onesrow", [1, P], F32)
    c_onesrow8 = din("c_onesrow8", [1, 8], F32)
    c_eps = din("c_eps", [1, 1], F32)
    c_iota16 = din("c_iota16", [P, 16], mybir.dt.int32)
    c_idpf = din("c_idpf", [P, 3], mybir.dt.int32)
    out_t = nc.dram_tensor("out_t", [TOK, 1024], F32, kind="ExternalOutput").ap()
    xt2d = nc.dram_tensor("xt2d", [TOK + 384, 1024], BF16).ap()
    w1b = nc.dram_tensor("w1b", [NE, 1024, 4096], BF16).ap()
    w2b = nc.dram_tensor("w2b", [NE, 4096, 1024], BF16).ap()
    idxd = nc.dram_tensor("idxd", [TOK + 384, 1], mybir.dt.int32).ap()

    xt_r = xt.rearrange("(c p) t -> p c t", p=P)

    with ExitStack() as st:
        E = st.enter_context
        S = lambda name, shape, dt: E(nc.sbuf_tensor(name, list(shape), dt))
        OUT = S("OUT", [P, DC, WT], F32)
        XLN2 = S("XLN2", [P, DC, WT], BF16)
        WB = S("WB", [P, 2, DC, 1024], BF16)
        W2B = S("W2B", [P, 2, NJ, 512], BF16)
        AWB = S("AWB", [P, DC, 1024], BF16)
        CAP = 384
        NCH = CAP // 128
        HB = S("HB", [P, NJ, CAP], BF16)
        POS = S("POS", [8, TOK], F32)
        SLT = S("SLT", [P, 16, 8], mybir.dt.int32)
        IDXC = S("IDXC", [P, 2, NCH], mybir.dt.int32)
        IOT16 = S("IOT16", [P, 16], mybir.dt.int32)
        IDPF = S("IDPF", [P, NCH], mybir.dt.int32)
        TMB = S("TMB", [P, 1024], BF16)
        XGT = S("XGT", [P, NCH, 1024], BF16)
        XGF = S("XGF", [P, 2, DC, CAP], BF16)
        YFM = S("YFM", [P, DC, CAP], F32)
        YTM = S("YTM", [P, 2, 1024], F32)
        TMBF = YTM[:, 0, :]
        QB = S("QB", [P, DC, WT], BF16)
        KDUP = S("KDUP", [P, NG, WT], BF16)
        VB = S("VB", [P, DC, WT], BF16)
        SA = S("SA", [P, DC, WT], BF16)
        EXPS = SA[:, 0:2, :]
        EXPS4 = SA[:, 0:4, :]
        ZC = S("ZC", [P, 2, 8], F32)
        ZACC = S("ZACC", [P, 4, NG], F32)
        DGS = S("DGS", [NG, WT], BF16)
        WRS = DGS
        DGT = S("DGT", [P, 4, NG], BF16)
        IDB = S("IDB", [P, P], BF16)
        SQB = S("SQB", [P, 1, WT], F32)
        XT1 = S("XT1", [P, 1, WT], F32)
        LGT = XT1[0:8, 0, :]
        ROWS = S("ROWS", [1, 3, WT], F32)
        RMX = S("RMX", [P, 1], F32)
        MKT = S("MKT", [P, 8], F32)
        MROW = S("MROW", [8, NR, WT], BF16)
        IDF = S("IDF", [P, P], F32)
        GMS = S("GMS", [P, DC, NG], BF16)
        GMB = S("GMB", [NG, DC, P], BF16)
        ONEC = S("ONEC", [P, 1], F32)
        ONER = S("ONER", [1, P], F32)
        ONER8 = ONER[0:1, 0:8]
        WRT = S("WRT", [P, 8, 8], F32)
        SMALL = S("SMALL", [8, 8], F32)
        BRC = SMALL[0:8, 0:1]
        BV = S("BV", [P, 8], F32); BO = S("BO", [P, 8], F32)
        BQS = S("BQS", [P, 8], F32); BKS = S("BKS", [P, 8], F32)
        B1C = S("B1C", [P, 256], F32)
        B2C = S("B2C", [P, 64], F32)
        SCOL = SMALL[0:8, 1:2]
        NSCOL = SMALL[0:8, 2:3]
        EPSC = SMALL[0:1, 3:4]

        PS = E(nc.psum_tensor("PS", [P, 8, 512], F32))

        sems = {}
        for snm in ("sINIT", "sX", "sKD", "sWA", "sW0", "sW1", "sW2", "sW3",
                    "sIX", "sGA", "sSC", "sT2D", "sXB", "sCST",
                    "sPE", "sACT", "sDVE", "sOUT"):
            sems[snm] = E(nc.semaphore(snm))
        blk = E(nc.Block())
        pl = Planner(sems)
        attn_state = {}

        init_loads = [
            (IDF, c_identf), (IDB, c_identbf), (GMS, c_gmsum), (GMB, c_gmb),
            (ONEC, c_onescol), (ONER, c_onesrow),
            (WRT, wrt), (BRC, brcol), (BQS, bqc), (BKS, bkc), (BV, bvc),
            (BO, boc), (B1C, b1c), (B2C, b2c), (EPSC, c_eps),
            (IOT16, c_iota16), (IDPF, c_idpf),
        ]
        for sb, src in init_loads:
            pl.op("sync", lambda e, sb=sb, src=src: e.dma_start(out=sb[:], in_=src),
                  inc="sINIT", amt=16, selfwait=False)
        v_init = pl.n["sINIT"]
        pl.wait("vector", "sINIT", v_init)
        pl.op("vector", lambda e: e.memset(
            XGT[:].rearrange("p c b -> p (c b)"), 0.0))
        vz = pl.op("vector", lambda e: e.memset(
            XGF[:].rearrange("p a c t -> p (a c t)"), 0.0))
        pl.wait("sync", "sDVE", vz)
        for j in range(NCH):
            pl.op("sync", lambda e, j=j: e.dma_start(
                out=xt2d[TOK + j * 128:TOK + (j + 1) * 128, :], in_=XGT[:, 0, :]),
                inc="sT2D", amt=16, selfwait=False)
        pl.op("vector", lambda e: e.tensor_scalar_mul(BQS[:], BQS[:], SCALE))
        pl.op("vector", lambda e: e.tensor_scalar_mul(BKS[:], BKS[:], SCALE))
        pl.wait("tensor", "sINIT", v_init)
        for kc in range(DC):
            v = pl.op("tensor", lambda e, kc=kc: e.matmul(
                PS[0:8, 0, 0:1], WRT[:, kc, :], ONEC[:],
                start=(kc == 0), stop=(kc == DC - 1)),
                inc="sPE" if kc == DC - 1 else None)
        pl.wait("vector", "sPE", v)
        pl.op("vector", lambda e: e.tensor_copy(SCOL[:], PS[0:8, 0, 0:1]))
        v = pl.op("vector", lambda e: e.tensor_scalar_mul(NSCOL[:], SCOL[:], -1.0))
        pl.consume_banks([0], "sDVE", v)

        def emit_ln(dst_bf, src3, banks, pe_wait, save_rows):
            bm, bs, bb1, bb2 = banks
            pl.use_banks("tensor", [bm, bs])
            if pe_wait is not None:
                pl.wait("tensor", pe_wait[0], pe_wait[1])
            for c in range(DC):
                par = c % 2
                sqt = SQB[:, 0, :] if par == 0 else XT1[:, 0, :]
                pl.use_buf("vector", ("SQB", par))
                v = pl.op("vector", lambda e, c=c, sqt=sqt: e.tensor_tensor(
                    sqt, src3[:, c, :], src3[:, c, :], ALU.mult))
                pl.wait("tensor", "sDVE", v)
                pl.op("tensor", lambda e, c=c, bm=bm: e.matmul(
                    PS[0:1, bm, :], ONEC[:], src3[:, c, :],
                    start=(c == 0), stop=(c == DC - 1)))
                vp = pl.op("tensor", lambda e, c=c, sqt=sqt, bs=bs: e.matmul(
                    PS[0:1, bs, :], ONEC[:], sqt,
                    start=(c == 0), stop=(c == DC - 1)),
                    inc="sPE")
                pl.hold_buf(("SQB", par), "sPE", vp)
            v_stats = pl.n["sPE"]
            pl.wait("vector", "sPE", v_stats)
            pl.op("vector", lambda e, bm=bm: e.tensor_scalar_mul(
                ROWS[0:1, 0, :], PS[0:1, bm, :], 1.0 / 1024.0))
            pl.op("vector", lambda e: e.tensor_tensor(
                ROWS[0:1, 1, :], ROWS[0:1, 0, :], ROWS[0:1, 0, :], ALU.mult))
            v = pl.op("vector", lambda e, bs=bs: e.scalar_tensor_tensor(
                ROWS[0:1, 1, :], PS[0:1, bs, :], 1.0 / 1024.0, ROWS[0:1, 1, :],
                ALU.mult, ALU.subtract))
            pl.consume_banks([bm, bs], "sDVE", v)
            pl.wait("scalar", "sDVE", v)
            pl.op("scalar", lambda e: e.activation(
                ROWS[0:1, 1, :], ROWS[0:1, 1, :], AF.Ln, bias=EPSC[0:1, 0:1]))
            va = pl.op("scalar", lambda e: e.activation(
                ROWS[0:1, 2, :], ROWS[0:1, 1, :], AF.Exp, scale=-0.5))
            if save_rows:
                pl.wait("vector", "sACT", va)
                pl.op("vector", lambda e: e.tensor_tensor(
                    ROWS[0:1, 1, :], ROWS[0:1, 0, :], ROWS[0:1, 2, :], ALU.mult))
            pl.wait("tensor", "sACT", va)
            pl.use_banks("tensor", [bb1, bb2])
            pl.op("tensor", lambda e, bb1=bb1: e.matmul(
                PS[:, bb1, :], ONER[:], ROWS[0:1, 0, :], start=True, stop=True))
            v = pl.op("tensor", lambda e, bb2=bb2: e.matmul(
                PS[:, bb2, :], ONER[:], ROWS[0:1, 2, :], start=True, stop=True),
                inc="sPE")
            pl.wait("vector", "sPE", v)
            for c in range(DC):
                pl.op("vector", lambda e, c=c, bb1=bb1: e.tensor_tensor(
                    XT1[:, 0, :], src3[:, c, :], PS[:, bb1, :], ALU.subtract))
                v = pl.op("vector", lambda e, c=c, bb2=bb2: e.tensor_tensor(
                    dst_bf[:, c, :], XT1[:, 0, :], PS[:, bb2, :], ALU.mult))
            pl.consume_banks([bb1, bb2], "sDVE", v)
            return v

        def emit_router(rr, v_ln2):
            sl = rr
            pl.use_banks("tensor", [4, 5, 6])
            pl.wait("tensor", "sDVE", v_ln2)
            for kc in range(DC):
                pl.op("tensor", lambda e, kc=kc, sl=sl: e.matmul(
                    PS[0:8, 4, :], WRT[:, kc, :], OUT[:, kc, :],
                    start=(kc == 0), stop=(kc == DC - 1)))
            pl.op("tensor", lambda e: e.matmul(
                PS[0:8, 5, :], ONER8[:], ROWS[0:1, 2, :], start=True, stop=True))
            v = pl.op("tensor", lambda e: e.matmul(
                PS[0:8, 6, :], ONER8[:], ROWS[0:1, 1, :], start=True, stop=True),
                inc="sPE")
            pl.wait("vector", "sPE", v)
            pl.op("vector", lambda e: e.tensor_copy(LGT[:], PS[0:8, 4, :]))
            pl.op("vector", lambda e: e.tensor_tensor(
                LGT[:], LGT[:], PS[0:8, 5, :], ALU.mult))
            pl.op("vector", lambda e: e.scalar_tensor_tensor(
                LGT[:], PS[0:8, 6, :], NSCOL[:], LGT[:], ALU.mult, ALU.add))
            v = pl.op("vector", lambda e: e.tensor_scalar(
                LGT[:], LGT[:], BRC[:], None, ALU.add))
            pl.consume_banks([4, 5, 6], "sDVE", v)
            pl.use_buf("vector", ("MROW", sl))
            for tc in range(4):
                ts = slice(tc * P, (tc + 1) * P)
                pl.use_banks("tensor", [7])
                pl.wait("tensor", "sDVE", v)
                vt = pl.op("tensor", lambda e, ts=ts: e.transpose(
                    PS[:, 7, 0:8], LGT[0:8, ts], IDF[0:8, 0:8]), inc="sPE")
                pl.wait("vector", "sPE", vt)
                vv = pl.op("vector", lambda e: e.reduce_max(
                    RMX[:], PS[:, 7, 0:8], axis=AX))
                pl.wait("vector", "sDVE", vv)
                vv = pl.op("vector", lambda e: e.tensor_scalar(
                    MKT[:], PS[:, 7, 0:8], RMX[:], None, ALU.is_ge))
                pl.consume_banks([7], "sDVE", vv)
                pl.use_banks("tensor", [7])
                pl.wait("tensor", "sDVE", vv)
                vt = pl.op("tensor", lambda e: e.transpose(
                    PS[0:8, 7, 0:P], MKT[:], IDF[:]), inc="sPE")
                pl.wait("vector", "sPE", vt)
                vv = pl.op("vector", lambda e, sl=sl, tc=tc: e.tensor_copy(
                    MROW[0:8, sl, tc * P:(tc + 1) * P], PS[0:8, 7, 0:P]))
                pl.consume_banks([7], "sDVE", vv)
            attn_state[("mrow", sl)] = pl.n["sDVE"]


        def emit_gather_dma(e_i):
            es = e_i % 2
            pl.wait("gpsimd", "sDVE", attn_state["slt"])
            pl.wait("gpsimd", "sIX", pl.n["sIX"])
            pl.op("gpsimd", lambda e: e.dma_start(
                out=idxd[0:CAP, :].rearrange("(p a) x -> p (a x)", p=P), in_=IDPF[:]),
                inc="sIX", amt=16, selfwait=False)
            vix = pl.n["sIX"]
            pl.wait("gpsimd", "sIX", vix)
            for tc in range(16):
                pl.op("gpsimd", lambda e, tc=tc, e_i=e_i: e.indirect_dma_start(
                    out=idxd, out_offset=IndirectOffsetOnAxis(SLT[:, tc, e_i:e_i + 1], 0),
                    in_=IOT16[:, tc:tc + 1], in_offset=None),
                    inc="sIX", amt=16, selfwait=False)
            vix = pl.n["sIX"]
            pl.wait("gpsimd", "sIX", vix)
            pl.use_buf("gpsimd", ("IDXC", es))
            for ch in range(NCH):
                pl.op("gpsimd", lambda e, es=es, ch=ch: e.dma_start(
                    out=IDXC[:, es, ch:ch + 1], in_=idxd[ch * 128:(ch + 1) * 128, :]),
                    inc="sIX", amt=16, selfwait=False)
            vix = pl.n["sIX"]
            pl.wait("gpsimd", "sIX", vix)
            pl.wait("gpsimd", "sT2D", pl.n["sT2D"])
            pl.use_buf("gpsimd", ("XGT",))
            pl.wait("gpsimd", "sGA", pl.n["sGA"])
            for ch in range(NCH):
                pl.op("gpsimd", lambda e, es=es, ch=ch: e.indirect_dma_start(
                    out=XGT[:, ch, :], out_offset=None,
                    in_=xt2d, in_offset=IndirectOffsetOnAxis(IDXC[:, es, ch:ch + 1], 0)),
                    inc="sGA", amt=16, selfwait=False)
            attn_state[("ga", e_i)] = pl.n["sGA"]

        def emit_gather_tp(e_i):
            es = e_i % 2
            v_ga = attn_state[("ga", e_i)]
            PSB4 = PS[:, 0, :].bitcast(BF16)
            pl.use_buf("tensor", ("XGF", es))
            for ch in range(NCH):
                pl.use_banks("tensor", [0])
                pl.wait("tensor", "sGA", v_ga)
                vpe = None
                for c in range(DC):
                    vpe = pl.op("tensor", lambda e, ch=ch, c=c, PSB4=PSB4: e.transpose(
                        PSB4[:, c * 128:(c + 1) * 128],
                        XGT[:, ch, c * 128:(c + 1) * 128], IDB[:]),
                        inc="sPE" if c == DC - 1 else None)
                pl.wait("vector", "sPE", vpe)
                vv = pl.op("vector", lambda e, es=es, ch=ch, PSB4=PSB4: e.tensor_copy(
                    XGF[:, es, :, ch * 128:(ch + 1) * 128],
                    PSB4[:, 0:1024].rearrange("p (c t) -> p c t", c=DC)))
                pl.consume_banks([0], "sDVE", vv)
            pl.hold_buf(("XGT",), "sPE", pl.n["sPE"])
            attn_state[("xg", e_i)] = vv
            return vv

        def weight_caster():
            for e_i in range(NE):
                for t, dst, rch in ((w1t, w1b, 128), (w2t, w2b, 512)):
                    for ch in range(8):
                        pl.op("gpsimd", lambda e, t=t, dst=dst, e_i=e_i,
                              ch=ch, rch=rch: e.dma_start(
                                  out=dst[e_i][ch * rch:(ch + 1) * rch, :],
                                  in_=t[e_i][ch * rch:(ch + 1) * rch, :]),
                              inc="sCST", amt=16, selfwait=False)
                        yield 1

        def attn_units(w, solo=False):
            pl.use_buf("sync", ("OUT", "pe"))
            pl.use_buf("sync", ("OUT", "dve"))
            pl.wait("sync", "sX", pl.n["sX"])
            vx = pl.op("sync", lambda e, w=w: e.dma_start(
                out=OUT[:], in_=xt_r[:, :, w * WT:(w + 1) * WT]),
                inc="sX", amt=16, selfwait=False)
            pl.use_buf("gpsimd", ("AWB",))
            pl.wait("gpsimd", "sWA", pl.n["sWA"])
            vwa = pl.op("gpsimd", lambda e: e.dma_start(
                out=AWB[:], in_=wq.rearrange("(c p) m -> p c m", p=P)),
                inc="sWA", amt=16, selfwait=False)
            yield 0.5
            pl.use_buf("vector", ("SA",))
            pl.wait("vector", "sX", vx)
            v_xln1 = emit_ln(SA, OUT[:], (6, 7, 6, 7), ("sX", vx), False)
            yield 2.0

            def proj(vw, dst_kind):
                pl.wait("tensor", "sWA", vw)
                pl.wait("tensor", "sDVE", v_xln1)
                vlast = vv = None
                for mc in range(DC):
                    bank = (2, 3, 6, 7)[mc % 4]
                    pl.use_banks("tensor", [bank])
                    for kc in range(DC):
                        vpe = pl.op("tensor",
                                    lambda e, mc=mc, kc=kc, bank=bank: e.matmul(
                                        PS[:, bank, :],
                                        AWB[:, kc, mc * 128:(mc + 1) * 128],
                                        SA[:, kc, :],
                                        start=(kc == 0), stop=(kc == DC - 1)),
                                    inc="sPE" if kc == DC - 1 else None)
                    pl.wait("vector", "sPE", vpe)
                    if dst_kind == "Q":
                        vv = pl.op("vector", lambda e, mc=mc, bank=bank: e.tensor_scalar(
                            QB[:, mc, :], PS[:, bank, :], SCALE, BQS[:, mc:mc + 1],
                            ALU.mult, ALU.add))
                    elif dst_kind == "K":
                        pl.op("vector", lambda e, mc=mc, bank=bank: e.tensor_scalar(
                            KDUP[0:64, 2 * mc, :], PS[0:64, bank, :], SCALE,
                            BKS[0:64, mc:mc + 1], ALU.mult, ALU.add))
                        vv = pl.op("vector", lambda e, mc=mc, bank=bank: e.tensor_scalar(
                            KDUP[64:128, 2 * mc + 1, :], PS[64:128, bank, :], SCALE,
                            BKS[64:128, mc:mc + 1], ALU.mult, ALU.add))
                    else:
                        vv = pl.op("vector", lambda e, mc=mc, bank=bank: e.tensor_scalar(
                            VB[:, mc, :], PS[:, bank, :], BV[:, mc:mc + 1], None,
                            ALU.add))
                    pl.consume_banks([bank], "sDVE", vv)
                    vlast = vpe
                return vlast, vv

            vq_last, _ = proj(vwa, "Q")
            pl.wait("gpsimd", "sPE", vq_last)
            pl.wait("gpsimd", "sWA", pl.n["sWA"])
            vwa = pl.op("gpsimd", lambda e: e.dma_start(
                out=AWB[:], in_=wk.rearrange("(c p) m -> p c m", p=P)),
                inc="sWA", amt=16, selfwait=False)
            yield 2.0
            vk_last, v_kevac = proj(vwa, "K")
            pl.wait("sync", "sDVE", v_kevac)
            pl.wait("sync", "sKD", pl.n["sKD"])
            pl.op("sync", lambda e: e.dma_start(
                out=KDUP[64:128, 0:NG:2, :], in_=KDUP[0:64, 0:NG:2, :]),
                inc="sKD", amt=16, selfwait=False)
            v_kdup = pl.op("sync", lambda e: e.dma_start(
                out=KDUP[0:64, 1:NG:2, :], in_=KDUP[64:128, 1:NG:2, :]),
                inc="sKD", amt=16, selfwait=False)
            pl.wait("gpsimd", "sPE", vk_last)
            pl.wait("gpsimd", "sWA", pl.n["sWA"])
            vwa = pl.op("gpsimd", lambda e: e.dma_start(
                out=AWB[:], in_=wv.rearrange("(c p) m -> p c m", p=P)),
                inc="sWA", amt=16, selfwait=False)
            yield 2.0
            vv_last, v_qkv = proj(vwa, "V")
            pl.wait("gpsimd", "sPE", vv_last)
            pl.wait("gpsimd", "sWA", pl.n["sWA"])
            vwo = pl.op("gpsimd", lambda e: e.dma_start(
                out=AWB[:], in_=wo.rearrange("(c p) m -> p c m", p=P)),
                inc="sWA", amt=16, selfwait=False)
            attn_state["vwo"] = vwo
            yield 2.0
            pl.wait("tensor", "sDVE", v_qkv)
            pl.wait("tensor", "sKD", v_kdup)
            pc = 0
            ng_sz = 4 if solo else 2
            for gb in range(NG):
                for t0 in range(4):
                    par = (gb * 4 + t0) % 2
                    pl.use_buf("scalar", ("ZC", par))
                    for bt in range(NG // ng_sz):
                        if solo:
                            bp = (0, 4)[pc % 2]
                        else:
                            bp = (2, 6)[pc % 2]
                        banks = list(range(bp, bp + ng_sz))
                        pl.use_banks("tensor", banks)
                        for gi in range(ng_sz):
                            ga = bt * ng_sz + gi
                            vpe = pl.op("tensor",
                                        lambda e, gb=gb, t0=t0, ga=ga, bp=bp, gi=gi: e.matmul(
                                            PS[:, bp + gi, :],
                                            KDUP[(ga % 2) * 64:(ga % 2) * 64 + 64, gb,
                                                 t0 * 128:(t0 + 1) * 128],
                                            QB[(ga % 2) * 64:(ga % 2) * 64 + 64,
                                               ga // 2, :],
                                            start=True, stop=True,
                                            tile_position=((ga % 2) * 64, 0)),
                                        inc="sPE" if gi == ng_sz - 1 else None)
                        pl.wait("scalar", "sPE", vpe)
                        va = pl.op("scalar",
                                   lambda e, bp=bp, par=par, bt=bt, ng_sz=ng_sz: e.activation(
                                       (EXPS4 if ng_sz == 4 else EXPS)[:]
                                       .rearrange("p a b -> p (a b)"),
                                       PS[:, bp:bp + ng_sz, :].rearrange("p a b -> p (a b)"),
                                       AF.Exp, accum_out=ZC[:, par, bt:bt + 1]))
                        pl.consume_banks(banks, "sACT", va)
                        pc += 1
                        yield 1.29 * ng_sz / 2
                    pl.wait("vector", "sACT", pl.n["sACT"])
                    vv = pl.op("vector", lambda e, gb=gb, t0=t0, par=par,
                               nb=NG // ng_sz: e.reduce_sum(
                        ZACC[:, t0, gb:gb + 1], ZC[:, par, 0:nb], axis=AX))
                    pl.hold_buf(("ZC", par), "sDVE", vv)
            attn_state["v_scores_pe"] = pl.n["sPE"]
            attn_state["v_exp_all"] = pl.n["sACT"]

        def attn_tail(w):
            sl = w % 2
            pl.wait("vector", "sPE", attn_state["v_scores_pe"])
            pl.wait("vector", "sACT", attn_state["v_exp_all"])
            v_prod = None
            for c in range(DC):
                pl.op("vector", lambda e, c=c: e.tensor_tensor(
                    SA[0:64, c, :], QB[0:64, c, :], KDUP[0:64, 2 * c, :], ALU.mult))
                v_prod = pl.op("vector", lambda e, c=c: e.tensor_tensor(
                    SA[64:128, c, :], QB[64:128, c, :], KDUP[64:128, 2 * c + 1, :],
                    ALU.mult))
            pl.wait("tensor", "sDVE", v_prod)
            pl.use_banks("tensor", [0])
            for c in range(DC):
                vpe = pl.op("tensor", lambda e, c=c: e.matmul(
                    PS[0:16, 0, :], GMS[:, c, :], SA[:, c, :],
                    start=(c == 0), stop=(c == DC - 1)),
                    inc="sPE" if c == DC - 1 else None)
            pl.wait("vector", "sPE", vpe)
            vv = pl.op("vector", lambda e: e.tensor_copy(DGS[:], PS[0:16, 0, :]))
            pl.consume_banks([0], "sDVE", vv)
            pl.use_banks("tensor", [1])
            pl.wait("tensor", "sDVE", vv)
            PSB1 = PS[:, 1, :].bitcast(BF16)
            for t0 in range(4):
                vpe = pl.op("tensor", lambda e, t0=t0, PSB1=PSB1: e.transpose(
                    PSB1[:, t0 * NG:(t0 + 1) * NG],
                    DGS[0:16, t0 * 128:(t0 + 1) * 128], IDB[0:16, 0:16]),
                    inc="sPE" if t0 == 3 else None)
            pl.wait("scalar", "sPE", vpe)
            va = pl.op("scalar", lambda e, PSB1=PSB1: e.activation(
                DGT[:].rearrange("p a b -> p (a b)"), PSB1[:, 0:64], AF.Exp))
            pl.consume_banks([1], "sACT", va)
            pl.wait("vector", "sACT", va)
            pl.op("vector", lambda e: e.reciprocal(
                ZACC[:].rearrange("p a b -> p (a b)"),
                ZACC[:].rearrange("p a b -> p (a b)")))
            vv = pl.op("vector", lambda e: e.tensor_tensor(
                DGT[:].rearrange("p a b -> p (a b)"),
                DGT[:].rearrange("p a b -> p (a b)"),
                ZACC[:].rearrange("p a b -> p (a b)"), ALU.mult))
            pl.wait("tensor", "sDVE", vv)
            pl.use_banks("tensor", [2])
            PSB2 = PS[:, 2, :].bitcast(BF16)
            for t0 in range(4):
                vpe = pl.op("tensor", lambda e, t0=t0, PSB2=PSB2: e.transpose(
                    PSB2[0:16, t0 * 128:(t0 + 1) * 128], DGT[:, t0, :], IDB[:]),
                    inc="sPE" if t0 == 3 else None)
            pl.wait("vector", "sPE", vpe)
            for t0 in range(4):
                vv = pl.op("vector", lambda e, t0=t0, PSB2=PSB2: e.tensor_copy(
                    WRS[0:16, t0 * 128:(t0 + 1) * 128],
                    PSB2[0:16, t0 * 128:(t0 + 1) * 128]))
            pl.consume_banks([2], "sDVE", vv)
            v_wrows = vv
            pl.wait("tensor", "sDVE", v_wrows)
            for c in range(DC):
                bank = 3 if c % 2 == 0 else 5
                pl.use_banks("tensor", [bank])
                vpe = pl.op("tensor", lambda e, c=c, bank=bank: e.matmul(
                    PS[:, bank, :], GMB[:, c, :], WRS[:], start=True, stop=True),
                    inc="sPE")
                pl.wait("vector", "sPE", vpe)
                vv = pl.op("vector", lambda e, c=c, bank=bank: e.tensor_tensor(
                    SA[:, c, :], PS[:, bank, :], VB[:, c, :], ALU.mult))
                pl.consume_banks([bank], "sDVE", vv)
            v_wv = vv
            pl.wait("tensor", "sWA", attn_state["vwo"])
            pl.wait("tensor", "sDVE", v_wv)
            for mc in range(DC):
                bank = 6 + mc % 2
                pl.use_banks("tensor", [bank])
                for kc in range(DC):
                    vpe = pl.op("tensor", lambda e, mc=mc, kc=kc, bank=bank: e.matmul(
                        PS[:, bank, :], AWB[:, kc, mc * 128:(mc + 1) * 128],
                        SA[:, kc, :], start=(kc == 0), stop=(kc == DC - 1)),
                        inc="sPE" if kc == DC - 1 else None)
                pl.wait("vector", "sPE", vpe)
                vv = pl.op("vector", lambda e, mc=mc, bank=bank:
                           e.scalar_tensor_tensor(
                               OUT[:, mc, :], PS[:, bank, :], BO[:, mc:mc + 1],
                               OUT[:, mc, :], ALU.add, ALU.add))
                pl.consume_banks([bank], "sDVE", vv)
                if mc == DC - 1:
                    pl.hold_buf(("AWB",), "sPE", vpe)
            v_x1 = vv
            pl.hold_buf(("SA",), "sPE", pl.n["sPE"])
            pl.wait("tensor", "sDVE", v_x1)
            PSRF = PS[:, 6:8, :]
            for tc in range(4):
                pl.use_banks("tensor", [6, 7])
                vpe = None
                for c in range(DC):
                    vpe = pl.op("tensor", lambda e, tc=tc, c=c, PSRF=PSRF: e.transpose(
                        PSRF[:, c // 4, (c % 4) * 128:(c % 4 + 1) * 128],
                        OUT[:, c, tc * 128:(tc + 1) * 128], IDF[:]),
                        inc="sPE" if c == DC - 1 else None)
                pl.wait("vector", "sPE", vpe)
                pl.wait("vector", "sXB", pl.n["sXB"])
                vv = pl.op("vector", lambda e, PSRF=PSRF: e.tensor_copy(
                    TMBF[:], PSRF[:, :, :].rearrange("p a t -> p (a t)")))
                pl.consume_banks([6, 7], "sDVE", vv)
                pl.wait("sync", "sDVE", vv)
                pl.wait("sync", "sXB", pl.n["sXB"])
                pl.op("sync", lambda e, w=w, tc=tc: e.dma_start(
                    out=out_t[w * WT + tc * 128:w * WT + (tc + 1) * 128, :],
                    in_=TMBF[:]),
                    inc="sXB", amt=16, selfwait=False)
            v_x1pe = pl.n["sPE"]
            pl.use_buf("vector", ("XLN2",))
            v_ln2 = emit_ln(XLN2[:], OUT[:], (0, 1, 2, 3),
                            ("sDVE", v_x1), True)
            emit_router(w, v_ln2)
            PSB3 = PS[:, 2, :].bitcast(BF16)
            for tc in range(4):
                pl.use_banks("tensor", [2])
                vpe = None
                for c in range(DC):
                    vpe = pl.op("tensor", lambda e, tc=tc, c=c, PSB3=PSB3: e.transpose(
                        PSB3[:, c * 128:(c + 1) * 128],
                        XLN2[:, c, tc * P:(tc + 1) * P], IDB[:]),
                        inc="sPE" if c == DC - 1 else None)
                pl.wait("vector", "sPE", vpe)
                pl.wait("vector", "sT2D", pl.n["sT2D"])
                vv = pl.op("vector", lambda e, PSB3=PSB3: e.tensor_copy(
                    TMB[:], PSB3[:, 0:1024]))
                pl.consume_banks([2], "sDVE", vv)
                pl.wait("sync", "sDVE", vv)
                pl.wait("sync", "sT2D", pl.n["sT2D"])
                pl.op("sync", lambda e, w=w, tc=tc: e.dma_start(
                    out=xt2d[w * WT + tc * P:w * WT + (tc + 1) * P, :], in_=TMB[:]),
                    inc="sT2D", amt=16, selfwait=False)
            pl.hold_buf(("XLN2",), "sPE", pl.n["sPE"])
            pl.hold_buf(("OUT", "pe"), "sPE", pl.n["sPE"])
            pl.hold_buf(("OUT", "dve"), "sDVE", pl.n["sDVE"])

        def slot_build():
            MRV = MROW[0:8, :, :].rearrange("p a b -> p (a b)")
            ZROW = XGT[0:8, :, :].rearrange("p a b -> p (a b)")[:, 0:TOK]
            POSN = PS[0:8, 4:8, :].rearrange("p a b -> p (a b)")
            pl.use_banks("vector", [4, 5, 6, 7])
            pl.op("vector", lambda e, MRV=MRV, ZROW=ZROW: e.tensor_tensor_scan(
                POS[:], MRV, ZROW, 0.0, ALU.add, ALU.add))
            pl.op("vector", lambda e, MRV=MRV, POSN=POSN: e.tensor_scalar(
                POSN, MRV, -1.0, 1.0, ALU.mult, ALU.add))
            pl.op("vector", lambda e, ZROW=ZROW, POSN=POSN: e.tensor_tensor_scan(
                POSN, POSN, ZROW, 0.0, ALU.add, ALU.add))
            pl.op("vector", lambda e, POSN=POSN: e.tensor_scalar(
                POSN, POSN, CAP - 1.0, None, ALU.add))
            pl.op("vector", lambda e: e.tensor_scalar(
                POS[:], POS[:], -1.0, None, ALU.add))
            pl.op("vector", lambda e, POSN=POSN: e.tensor_tensor(
                POS[:], POS[:], POSN, ALU.subtract))
            pl.op("vector", lambda e, MRV=MRV: e.tensor_tensor(
                POS[:], POS[:], MRV, ALU.mult))
            v_pos = pl.op("vector", lambda e, POSN=POSN: e.tensor_tensor(
                POS[:], POS[:], POSN, ALU.add))
            pl.consume_banks([4, 5, 6, 7], "sDVE", v_pos)
            pl.wait("tensor", "sDVE", v_pos)
            for tc in range(16):
                bank = tc % 2
                pl.use_banks("tensor", [bank])
                vpe = pl.op("tensor", lambda e, tc=tc, bank=bank: e.transpose(
                    PS[0:128, bank, 0:8], POS[0:8, tc * P:(tc + 1) * P],
                    IDF[0:8, 0:8]), inc="sPE")
                pl.wait("vector", "sPE", vpe)
                vv = pl.op("vector", lambda e, tc=tc, bank=bank: e.tensor_copy(
                    SLT[:, tc, :], PS[:, bank, 0:8]))
                pl.consume_banks([bank], "sDVE", vv)
            attn_state["slt"] = pl.n["sDVE"]

        def ffn_expert(e_i):
            if True:
                if e_i + 1 < NE:
                    emit_gather_dma(e_i + 1)
                vm = attn_state[("xg", e_i)]
                es = e_i % 2
                for q in range(NQ):
                    sq = q % 2
                    pl.use_buf("sync", ("WB", sq))
                    s1nm = "sW0" if sq == 0 else "sW1"
                    pl.wait("sync", s1nm, pl.n[s1nm])
                    pl.wait("sync", "sCST", 16 * 16 * NE)
                    vw1 = pl.op("sync", lambda e, e_i=e_i, q=q, sq=sq: e.dma_start(
                        out=WB[:, sq, :, :],
                        in_=w1b[e_i].rearrange("(c p) f -> p c f", p=P)
                            [:, :, q * 1024:(q + 1) * 1024]),
                        inc=s1nm, amt=16, selfwait=False)
                    vw2 = [None, None]
                    for hh in range(2):
                        pl.use_buf("sync", ("W2B", hh))
                        s2nm = "sW2" if hh == 0 else "sW3"
                        pl.wait("sync", s2nm, pl.n[s2nm])
                        vw2[hh] = pl.op("sync",
                                        lambda e, e_i=e_i, q=q, hh=hh: e.dma_start(
                                            out=W2B[:, hh, :, :],
                                            in_=w2b[e_i].rearrange("(j p) d -> p j d", p=P)
                                                [:, q * NJ:(q + 1) * NJ,
                                                 hh * 512:(hh + 1) * 512]),
                                        inc=s2nm, amt=16, selfwait=False)
                    hevac = [("sDVE", 0)] * NJ
                    pl.wait("tensor", s1nm, vw1)
                    pl.wait("tensor", "sDVE", vm)
                    vv = vpe = None
                    for j in range(NJ):
                        hbank = 2 + j % 2
                        pl.use_banks("tensor", [hbank])
                        for kc in range(DC):
                            vpe = pl.op("tensor",
                                        lambda e, j=j, kc=kc, hbank=hbank, sq=sq, es=es: e.matmul(
                                            PS[:, hbank, 0:CAP],
                                            WB[:, sq, kc, j * 128:(j + 1) * 128],
                                            XGF[:, es, kc, :],
                                            start=(kc == 0), stop=(kc == DC - 1)),
                                        inc="sPE" if kc == DC - 1 else None)
                        if j % 2 == 0:
                            pl.wait("vector", "sPE", vpe)
                            vv = pl.op("vector",
                                       lambda e, j=j, q=q, e_i=e_i, hbank=hbank: e.tensor_scalar(
                                           HB[:, j, :], PS[:, hbank, 0:CAP],
                                           B1C[:, e_i * 32 + q * NJ + j:
                                               e_i * 32 + q * NJ + j + 1],
                                           0.0, ALU.add, ALU.max))
                            pl.consume_banks([hbank], "sDVE", vv)
                            hevac[j] = ("sDVE", vv)
                        else:
                            pl.wait("scalar", "sPE", vpe)
                            vv = pl.op("scalar",
                                       lambda e, j=j, q=q, e_i=e_i, hbank=hbank: e.activation(
                                           HB[:, j, :], PS[:, hbank, 0:CAP], AF.Relu,
                                           bias=B1C[:, e_i * 32 + q * NJ + j:
                                                    e_i * 32 + q * NJ + j + 1]))
                            pl.consume_banks([hbank], "sACT", vv)
                            hevac[j] = ("sACT", vv)
                    pl.hold_buf(("WB", sq), "sPE", vpe)
                    for dr in range(4):
                        hh = dr // 2
                        pl.wait("tensor", "sW2" if hh == 0 else "sW3", vw2[hh])
                        yb = 0 if dr % 2 == 0 else 4
                        ybanks = [yb, yb + 1]
                        pl.use_banks("tensor", ybanks)
                        v_y = None
                        for j in range(NJ):
                            if dr == 0:
                                pl.wait("tensor", hevac[j][0], hevac[j][1])
                            for b2i in range(2):
                                dc = dr * 2 + b2i
                                v_y = pl.op("tensor",
                                            lambda e, j=j, dc=dc, b2i=b2i, hh=hh, yb=yb: e.matmul(
                                                PS[:, yb + b2i, 0:CAP],
                                                W2B[:, hh, j,
                                                    (dc % 4) * 128:(dc % 4 + 1) * 128],
                                                HB[:, j, :],
                                                start=(j == 0), stop=(j == NJ - 1)),
                                            inc="sPE" if (j == NJ - 1 and b2i == 1)
                                            else None)
                        pl.wait("vector", "sPE", v_y)
                        if q == 0:
                            for b2i in range(2):
                                dc = dr * 2 + b2i
                                vv = pl.op("vector",
                                           lambda e, dc=dc, yb=yb, b2i=b2i, e_i=e_i: e.tensor_scalar(
                                               YFM[:, dc, :], PS[:, yb + b2i, 0:CAP],
                                               B2C[:, e_i * 8 + dc:e_i * 8 + dc + 1],
                                               None, ALU.add))
                        else:
                            vv = pl.op("vector", lambda e, dr=dr, yb=yb: e.tensor_tensor(
                                YFM[:, dr * 2:(dr + 1) * 2, :],
                                YFM[:, dr * 2:(dr + 1) * 2, :],
                                PS[:, yb:yb + 2, 0:CAP], ALU.add))
                        pl.consume_banks(ybanks, "sDVE", vv)
                    pl.hold_buf(("W2B", 0), "sPE", pl.n["sPE"])
                    pl.hold_buf(("W2B", 1), "sPE", pl.n["sPE"])

                pl.hold_buf(("XGF", es), "sPE", pl.n["sPE"])
                PSY = PS[:, 4:6, :]
                for ch in range(NCH):
                    cp = ch % 2
                    pl.use_banks("tensor", [4, 5])
                    vpe = None
                    for c in range(DC):
                        vpe = pl.op("tensor", lambda e, ch=ch, c=c: e.transpose(
                            PSY[:, c // 4, (c % 4) * 128:(c % 4 + 1) * 128],
                            YFM[:, c, ch * 128:(ch + 1) * 128], IDF[:]),
                            inc="sPE" if c == DC - 1 else None)
                    pl.wait("vector", "sPE", vpe)
                    pl.use_buf("vector", ("YTM", cp))
                    vv = pl.op("vector", lambda e, cp=cp: e.tensor_copy(
                        YTM[:, cp, :],
                        PSY[:, :, :].rearrange("p a t -> p (a t)")))
                    pl.consume_banks([4, 5], "sDVE", vv)
                    pl.wait("gpsimd", "sDVE", vv)
                    pl.wait("gpsimd", "sXB", 16 * 16)
                    pl.wait("gpsimd", "sSC", pl.n["sSC"])
                    vsc = pl.op("gpsimd", lambda e, es=es, ch=ch, cp=cp: e.indirect_dma_start(
                        out=out_t, out_offset=IndirectOffsetOnAxis(IDXC[:, es, ch:ch + 1], 0),
                        in_=YTM[:, cp, :], in_offset=None,
                        bounds_check=TOK - 1, oob_is_err=False,
                        compute_op=ALU.add),
                        inc="sSC", amt=16, selfwait=False)
                    pl.hold_buf(("YTM", cp), "sSC", vsc)
                pl.hold_buf(("IDXC", es), "sSC", pl.n["sSC"])
                if e_i + 1 < NE:
                    emit_gather_tp(e_i + 1)

        caster = weight_caster()
        ticks = 0

        def tick():
            nonlocal ticks
            ticks += 1
            if ticks % 4 == 0:
                next(caster, None)

        for _ in attn_units(0, solo=True):
            tick()
        for w in range(NR):
            attn_tail(w)
            if w + 1 < NR:
                for _ in attn_units(w + 1, solo=True):
                    tick()
        for _ in caster:
            pass
        pl.hold_buf(("YTM", 0), "sXB", 16 * 4 * NR)
        slot_build()
        emit_gather_dma(0)
        emit_gather_tp(0)
        for e_i in range(NE):
            ffn_expert(e_i)
        pl.wait("sync", "sXB", 16 * 4 * NR)
        pl.wait("sync", "sSC", 16 * NCH * NE)

        pl.check()
        n_ops = {e: len(pl.q[e]) for e in ENGINES}
        pl.replay(blk)
    return nc, n_ops


_CACHE = {}


def _host_layout(inputs):
    bf = ml_dtypes.bfloat16
    f32 = np.float32
    X = np.asarray(inputs["L0_attn_X_embed"], f32)
    xt_full = np.ascontiguousarray(X.reshape(16384, 1024).T)

    def colsplit(b):
        return np.ascontiguousarray(np.asarray(b, f32).reshape(8, 128).T)

    common = dict(
        wq=np.ascontiguousarray(np.asarray(inputs["L0_attn_W_q"], f32)),
        wk=np.ascontiguousarray(np.asarray(inputs["L0_attn_W_k"], f32)),
        wv=np.ascontiguousarray(np.asarray(inputs["L0_attn_W_v"], f32)),
        wo=np.ascontiguousarray(np.asarray(inputs["L0_attn_W_o"], f32)),
        bqc=colsplit(inputs["L0_attn_b_q"]),
        bkc=colsplit(inputs["L0_attn_b_k"]),
        bvc=colsplit(inputs["L0_attn_b_v"]),
        boc=colsplit(inputs["L0_attn_b_o"]),
        wrt=np.ascontiguousarray(
            np.asarray(inputs["L0_ffn_Wr"], f32).T.reshape(8, 128, 8)
            .transpose(1, 0, 2)),
        brcol=np.ascontiguousarray(np.asarray(inputs["L0_ffn_br"], f32).reshape(8, 1)),
        b1c=np.ascontiguousarray(
            np.asarray(inputs["L0_ffn_b1"], f32).reshape(8, 32, 128)
            .transpose(2, 0, 1).reshape(128, 256)),
        b2c=np.ascontiguousarray(
            np.asarray(inputs["L0_ffn_b2"], f32).reshape(8, 8, 128)
            .transpose(2, 0, 1).reshape(128, 64)),
        w1t=np.ascontiguousarray(np.asarray(inputs["L0_ffn_W1"], f32).transpose(0, 2, 1)),
        w2t=np.ascontiguousarray(np.asarray(inputs["L0_ffn_W2"], f32).transpose(0, 2, 1)),
    )
    gm_sum = np.zeros((P, DC, NG), bf)
    gm_b = np.zeros((NG, DC, P), bf)
    for c in range(DC):
        for p in range(P):
            g = 2 * c + p // 64
            gm_sum[p, c, g] = 1
            gm_b[g, c, p] = 1
    sel = np.zeros((8, NE, P), bf)
    for e_i in range(NE):
        sel[e_i, e_i, :] = 1
    common.update(
        c_identf=np.eye(P, dtype=f32),
        c_identbf=np.eye(P, dtype=bf),
        c_gmsum=gm_sum, c_gmb=gm_b, c_sel=sel,
        c_onescol=np.ones((P, 1), f32),
        c_onesrow=np.ones((1, P), f32),
        c_onesrow8=np.ones((1, 8), f32),
        c_eps=np.full((1, 1), EPS, f32),
        c_iota16=np.ascontiguousarray(
            (np.arange(16)[None, :] * P + np.arange(P)[:, None]).astype(np.int32)),
        c_idpf=np.ascontiguousarray(
            (TOK + 3 * np.arange(P)[:, None] + np.arange(3)[None, :])
            .astype(np.int32)),
    )
    in_maps = []
    for c in range(8):
        m = dict(common)
        m["xt"] = np.ascontiguousarray(xt_full[:, c * TOK:(c + 1) * TOK])
        in_maps.append(m)
    return in_maps


def kernel(**inputs):
    if "nc" not in _CACHE:
        _CACHE["nc"] = build_nc()[0]
    nc = _CACHE["nc"]
    in_maps = _host_layout(inputs)
    res = run_bass_kernel_spmd(nc, in_maps, core_ids=list(range(8)))
    outs = [np.asarray(r["out_t"], dtype=np.float32) for r in res.results]
    return np.ascontiguousarray(np.concatenate(outs, axis=0)).reshape(2, 8192, 1024)
